# version 01 (final)
# baseline (speedup 1.0000x reference)
import math
import numpy as np
from contextlib import ExitStack

import concourse.bass as bass
import concourse.mybir as mybir
from concourse.bass_utils import run_bass_kernel_spmd
import ml_dtypes

F32 = mybir.dt.float32
BF16 = mybir.dt.bfloat16
I32 = mybir.dt.int32
AF = mybir.ActivationFunctionType
ALU = mybir.AluOpType
AX = mybir.AxisListType

B, S, D, H, HD = 4, 2048, 2048, 16, 128
TLOC = 1024
NTT = TLOC // 128
NC = D // 128
SCALE0 = 1.0 / math.sqrt(HD)
QK_ROPE, QK_NOPE, KV_LORA, V_DIM = 64, 128, 512, 128
QK_DIM = QK_ROPE + QK_NOPE
SCALE1 = 1.0 / math.sqrt(QK_DIM)
RMS_EPS = 1.1920928955078125e-07
LN_EPS = 1e-5
FFN_EPS = 1e-06

CHUNKS = {0: (0, 3), 1: (1, 2)}
NJ = (8, 16)
MASKED = set((0, j) for j in range(8)) | set((1, j) for j in range(8, 16))
MAGIC = 0x51000000


def _slopes():
    m = np.arange(1, H + 1, dtype=np.float64) * (8.0 / H)
    return 1.0 / 2.0 ** m


def host_constants(parity):
    out = {}
    slopes = _slopes()
    chunks = CHUNKS[parity]
    for slot in range(2):
        q0 = chunks[slot] * 512
        nj = NJ[slot]
        arr = np.zeros((nj, H, 128, 512), dtype=np.float64)
        arr1 = np.zeros((nj, 128, 512), dtype=np.float64)
        k_idx = np.arange(128)
        q_idx = np.arange(512)
        for j in range(nj):
            k0 = 128 * j
            rel = (k0 + k_idx[:, None]) - (q0 + q_idx[None, :])
            keep = rel <= 0
            arr1[j] = keep.astype(np.float64)
            for h in range(H):
                arr[j, h] = np.where(keep, np.exp(slopes[h] * rel), 0.0)
        out[f"t0_s{slot}"] = arr.astype(ml_dtypes.bfloat16)
        out[f"t1_s{slot}"] = arr1.astype(ml_dtypes.bfloat16)
    base = 10000.0
    freqs = 1.0 / base ** (np.arange(0, QK_ROPE, 2, dtype=np.float64) / QK_ROPE)
    pos = np.concatenate([np.arange(c * 512, (c + 1) * 512) for c in chunks])
    t = pos[None, :] * freqs[:, None]
    cosf = np.cos(t)
    sinf = np.sin(t)
    out["q_cos"] = cosf.astype(ml_dtypes.bfloat16)
    out["q_sin"] = sinf.astype(ml_dtypes.bfloat16)
    km = cosf.T.reshape(NTT, 128, 32).transpose(1, 0, 2).reshape(128, NTT * 32)
    ks = sinf.T.reshape(NTT, 128, 32).transpose(1, 0, 2).reshape(128, NTT * 32)
    out["k_cos"] = np.ascontiguousarray(km).astype(ml_dtypes.bfloat16)
    out["k_sin"] = np.ascontiguousarray(ks).astype(ml_dtypes.bfloat16)
    out["ident"] = np.eye(128, dtype=ml_dtypes.bfloat16)
    return out


class Prog:
    def __init__(self, nc, ctx):
        self.nc = nc
        self.ctx = ctx
        self.sems = {}
        self.counts = {}

    def sem(self, name):
        if name not in self.sems:
            self.sems[name] = self.ctx.enter_context(self.nc.semaphore(name))
        self.counts.setdefault(name, 0)
        return self.sems[name]

    def inc(self, instr, name, n=1):
        instr.then_inc(self.sem(name), n)
        self.counts[name] += n
        return self.counts[name]

    def bump(self, eng, name, n=1):
        ins = eng.nop()
        self.inc(ins, name, n)

    def multi_inc(self, eng, ins, pairs):
        (n0, c0) = pairs[0]
        self.inc(ins, n0, c0)
        if len(pairs) > 1:
            eng.wait_ge(self.sem(n0), self.counts[n0])
            for (nm, cc) in pairs[1:]:
                self.inc(eng.nop(), nm, cc)

    def wait(self, eng, name, lag=0):
        v = self.counts.get(name, 0) - lag
        if v > 0:
            eng.wait_ge(self.sem(name), v)


def spin_poll(P, eng, pflag_tile, src_ap, target, tag, sem_name=None):
    b = eng.bass
    engs = [eng.engine]
    cnt = b.alloc_registers(f"pc_{tag}", engines=engs)
    sem = P.sem(sem_name or f"poll_{tag}")
    b.regs_mov(cnt, 0)
    loopn = f"spin_{tag}"
    donen = f"spin_{tag}_done"
    b.br(loopn, engines=engs)
    with b.body(loopn, valid_engines=engs):
        b.regs_alu(cnt, cnt, 16, op=ALU.add)
        eng.dma_start(out=pflag_tile[0, :], in_=src_ap).then_inc(sem, 16)
        eng.wait_ge(sem, cnt)
        got = eng.value_load(pflag_tile[0:1, 0:1])
        b.br_cmp(got, target, on_true=donen, on_false=loopn, comp_op="IS_GE", engines=engs)
    b.switch_bb(donen)


def emit_rstd(P, block, ag, eps, tag):
    @block.scalar
    def _(scalar):
        P.wait(scalar, tag + "v")
        scalar.activation(out=ag[:, 1:2], in_=ag[:, 1:2], func=AF.Ln).then_inc(P.sem(tag + "s"), 1)
        P.counts[tag + "s"] += 1
        scalar.wait_ge(P.sem(tag + "s"), P.counts[tag + "s"])
        scalar.activation(out=ag[:, 1:2], in_=ag[:, 1:2], func=AF.Exp, scale=-0.5).then_inc(P.sem(tag + "s"), 1)
        P.counts[tag + "s"] += 1


def emit_ln_stats(P, block, src, st, ag, eps, tag, pre_wait=None):
    n = src.shape[1]
    nch = (n + 511) // 512
    @block.vector
    def _(vector):
        if pre_wait:
            for w in pre_wait:
                P.wait(vector, w)
        for i in range(nch):
            ins = vector.bn_stats(st[:, i * 6:(i + 1) * 6], src[:, i * 512:min(n, (i + 1) * 512)])
        ins.then_inc(P.sem(tag + "v"), 1)
        P.counts[tag + "v"] += 1
        vector.wait_ge(P.sem(tag + "v"), P.counts[tag + "v"])
        vector.bn_aggr(ag, st[:, 0:6 * nch]).then_inc(P.sem(tag + "v"), 1)
        P.counts[tag + "v"] += 1
        vector.wait_ge(P.sem(tag + "v"), P.counts[tag + "v"])
        vector.tensor_scalar_add(ag[:, 1:2], ag[:, 1:2], float(eps)).then_inc(P.sem(tag + "v"), 1)
        P.counts[tag + "v"] += 1
    emit_rstd(P, block, ag, eps, tag)


def emit_ln_and_fm(P, block, x_tm, xlnt, stats, trP, ident, dst_fm, eps, tag,
                   extra_wait=None):
    for tt in range(NTT):
        xt = x_tm[:, tt * D:(tt + 1) * D]
        st = stats[:, tt * 32:tt * 32 + 24]
        ag = stats[:, tt * 32 + 24:tt * 32 + 26]
        emit_ln_stats(P, block, xt, st, ag, eps, tag,
                      pre_wait=[extra_wait] if (extra_wait and tt == 0) else None)

        @block.vector
        def _(vector, xt=xt, ag=ag):
            P.wait(vector, tag + "s")
            P.wait(vector, tag + "e")
            ins = vector.tensor_scalar(out=xlnt[:, :], in0=xt, scalar1=ag[:, 0:1],
                                       scalar2=ag[:, 1:2], op0=ALU.subtract, op1=ALU.mult)
            P.inc(ins, tag + "a", 1)

        for cg in range(4):
            @block.tensor
            def _(tensor, tt=tt, cg=cg):
                P.wait(tensor, tag + "a")
                P.wait(tensor, tag + "e")
                for ci in range(4):
                    c = cg * 4 + ci
                    ins = tensor.transpose(trP[:, ci * 128:(ci + 1) * 128],
                                           xlnt[:, c * 128:(c + 1) * 128], ident[:, :])
                P.inc(ins, tag + "t", 1)

            @block.scalar
            def _(scalar, tt=tt, cg=cg):
                P.wait(scalar, tag + "t")
                for ci in range(4):
                    c = cg * 4 + ci
                    ins = scalar.copy(dst_fm[:, c * TLOC + tt * 128: c * TLOC + (tt + 1) * 128],
                                      trP[:, ci * 128:(ci + 1) * 128])
                P.inc(ins, tag + "e", 1)

    @block.scalar
    def _(scalar):
        P.wait(scalar, tag + "e")
        P.bump(scalar, tag + "fm", 1)



def build_l0_attn(debug=()):
    nc = bass.Bass("TRN2", target_bir_lowering=False, debug=False)
    x_in = nc.dram_tensor("x_in", [TLOC, D], F32, kind="ExternalInput").ap()
    wqkv = nc.dram_tensor("wqkv", [3 * D, D], F32, kind="ExternalInput").ap()
    bqkv = nc.dram_tensor("bqkv", [128, 48], F32, kind="ExternalInput").ap()
    wout = nc.dram_tensor("wout", [D, D], F32, kind="ExternalInput").ap()
    bout = nc.dram_tensor("bout", [1, D], F32, kind="ExternalInput").ap()
    t0_s0 = nc.dram_tensor("t0_s0", [NJ[0], H, 128, 512], BF16, kind="ExternalInput").ap()
    t0_s1 = nc.dram_tensor("t0_s1", [NJ[1], H, 128, 512], BF16, kind="ExternalInput").ap()
    ident_d = nc.dram_tensor("ident", [128, 128], BF16, kind="ExternalInput").ap()
    x_out = nc.dram_tensor("x_out", [TLOC, D], F32, kind="ExternalOutput").ap()

    k_hbm = nc.dram_tensor("k_hbm", [16, 128, H * 128], BF16, kind="Internal", addr_space="Shared").ap()
    v_hbm = nc.dram_tensor("v_hbm", [16, 128, D], BF16, kind="Internal", addr_space="Shared").ap()
    norm_sh = nc.dram_tensor("norm_sh", [2, 1, 4], F32, kind="Internal", addr_space="Shared").ap()
    k_loc = nc.dram_tensor("k_loc", [8, 128, H * 128], BF16, kind="Internal").ap()
    ncsc = nc.dram_tensor("ncsc", [1, 1], F32, kind="Internal").ap()
    v_loc = nc.dram_tensor("v_loc", [8, 128, D], BF16, kind="Internal").ap()
    flag_sh = nc.dram_tensor("flag_sh", [2, 1], I32, kind="Internal", addr_space="Shared").ap()

    dbg = {}
    for name, shp, dt_ in debug:
        dbg[name] = nc.dram_tensor("dbg_" + name, shp, dt_, kind="ExternalOutput").ap()

    with ExitStack() as ctx:
        E = ctx.enter_context
        P = Prog(nc, ctx)
        x_tm = E(nc.sbuf_tensor("x_tm", [128, NTT * D], F32))
        pA = E(nc.sbuf_tensor("pA", [128, NC * TLOC], BF16))
        pB = E(nc.sbuf_tensor("pB", [128, H * TLOC], BF16))
        pC = E(nc.sbuf_tensor("pC", [128, H * TLOC], BF16))
        pD = E(nc.sbuf_tensor("pD", [128, D], F32))
        pE = E(nc.sbuf_tensor("pE", [128, 2 * D], BF16))
        xlnt = E(nc.sbuf_tensor("xlnt", [128, D], BF16))
        stats = E(nc.sbuf_tensor("stats", [128, 256], F32))
        ident = E(nc.sbuf_tensor("identt", [128, 128], BF16))
        onescol = E(nc.sbuf_tensor("onescol", [128, 1], BF16))
        onesbr = E(nc.sbuf_tensor("onesbr", [1, 128], BF16))
        onesf = E(nc.sbuf_tensor("onesft", [1, 128], F32))
        bcol = E(nc.sbuf_tensor("bcol", [128, 48], F32))
        borowb = E(nc.sbuf_tensor("borowb", [1, D], BF16))
        vstg = E(nc.sbuf_tensor("vstg", [128, 1024], BF16))
        cvals = E(nc.sbuf_tensor("cvals", [1, 16], F32))
        negC = E(nc.sbuf_tensor("negC", [128, 1], F32))
        kst = E(nc.sbuf_tensor("kst", [128, 512], BF16))
        vst = E(nc.sbuf_tensor("vst", [128, 512], BF16))
        Tst = E(nc.sbuf_tensor("Tst", [128, 2048], BF16))
        pbuf = E(nc.sbuf_tensor("pbuf", [128, 2048], BF16))
        sums = E(nc.sbuf_tensor("sums", [1, 1024], F32))
        nrow = E(nc.sbuf_tensor("nrow", [1, 512], F32))
        rsum = E(nc.sbuf_tensor("rsum", [1, 512], F32))
        flag = E(nc.sbuf_tensor("flag", [1, 1], I32))
        pflag = E(nc.sbuf_tensor("pflag", [1, 1], I32))
        sqs = pE[:, 0:512]
        rtile = pD[:, 0:512]
        pnorm = E(nc.sbuf_tensor("pnorm", [1, 4], F32))

        psA = E(nc.psum_tensor("psA", [128, 1024], F32))
        psB = E(nc.psum_tensor("psB", [128, 1024], F32))
        ctxP = E(nc.psum_tensor("ctxP", [128, 1024], F32))
        misc = E(nc.psum_tensor("misc", [128, 512], F32))
        trP = E(nc.psum_tensor("trP", [128, 512], BF16))

        block = E(nc.Block())

        @block.sync
        def _(sync):
            ins = sync.dma_start(out=x_tm[:, :].rearrange("p (tt d) -> p tt d", d=D), in_=x_in.rearrange("(tt p) d -> p tt d", p=128))
            P.inc(ins, "ld", 16)
            ins = sync.dma_start(out=ident[:, :], in_=ident_d)
            P.inc(ins, "ld", 16)
            ins = sync.dma_start(out=pD[0:1, 0:D], in_=bout)
            P.inc(ins, "ld", 16)
            ins = sync.dma_start(out=bcol[:, :], in_=bqkv)
            P.inc(ins, "ld", 16)

        @block.vector
        def _(vector):
            P.wait(vector, "ld")
            vector.memset(onesf[:, :], 1.0)
            vector.memset(onescol[:, :], 1.0)
            vector.memset(onesbr[:, :], 1.0)
            vector.memset(sums[:, :], 0.0)
            vector.memset(cvals[:, :], 0.0)
            ins = vector.tensor_copy(borowb[:, :], pD[0:1, 0:D])
            P.inc(ins, "cst", 1)

        emit_ln_and_fm(P, block, x_tm, xlnt, stats, trP, ident, pA, LN_EPS,
                       tag="l0", extra_wait="ld")

        pc4 = pC[:, :].rearrange("p (blk h t) -> p blk h t", blk=8, h=H)

        for oc in range(48):
            @block.sync
            def _(sync, oc=oc):
                P.wait(sync, "wfree")
                if oc == 0:
                    P.wait(sync, "cst")
                ins = sync.dma_start(out=pD[:, :], in_=wqkv[oc * 128:(oc + 1) * 128, :])
                P.inc(ins, "wld", 16)

            @block.vector
            def _(vector, oc=oc):
                P.wait(vector, "wld")
                P.wait(vector, "wev")
                if oc == 0:
                    P.wait(vector, "cst")
                ins = vector.tensor_copy(pE[:, 0:D], pD[:, :])
                P.inc(ins, "wcv", 1)

            for cg in range(4):
                @block.tensor
                def _(tensor, oc=oc, cg=cg):
                    P.wait(tensor, "wcv")
                    P.wait(tensor, "wev")
                    P.wait(tensor, "vte")
                    if oc == 0 and cg == 0:
                        P.wait(tensor, "l0fm")
                    for ci in range(4):
                        c = cg * 4 + ci
                        ins = tensor.transpose(trP[:, ci * 128:(ci + 1) * 128],
                                               pE[:, c * 128:(c + 1) * 128], ident[:, :])
                    P.inc(ins, "wtp", 1)

                @block.scalar
                def _(scalar, cg=cg):
                    P.wait(scalar, "wtp")
                    ins = scalar.copy(pE[:, D + cg * 512: D + (cg + 1) * 512], trP[:, 0:512])
                    P.inc(ins, "wev", 1)

            @block.tensor
            def _(tensor, oc=oc):
                P.wait(tensor, "wev")
                P.wait(tensor, "qkev")
                for c in range(NC):
                    lhsT = pE[:, D + c * 128: D + (c + 1) * 128]
                    tensor.matmul(psA[:, 0:512], lhsT, pA[:, c * TLOC: c * TLOC + 512],
                                  start=(c == 0), stop=(c == NC - 1))
                    ins = tensor.matmul(psA[:, 512:1024], lhsT, pA[:, c * TLOC + 512:(c + 1) * TLOC],
                                        start=(c == 0), stop=(c == NC - 1))
                P.inc(ins, "mmq", 1)

            if oc < 16:
                @block.scalar
                def _(scalar, oc=oc):
                    P.wait(scalar, "mmq")
                    scalar.activation(out=pB[:, oc * TLOC: oc * TLOC + 512], in_=psA[:, 0:512],
                                      func=AF.Identity, bias=bcol[:, oc:oc + 1])
                    ins = scalar.activation(out=pB[:, oc * TLOC + 512:(oc + 1) * TLOC],
                                            in_=psA[:, 512:1024], func=AF.Identity, bias=bcol[:, oc:oc + 1])
                    P.multi_inc(scalar, ins, [("qkev", 1), ("wfree", 1)])
            elif oc < 32:
                h = oc - 16

                @block.scalar
                def _(scalar, h=h, oc=oc):
                    P.wait(scalar, "mmq")
                    for s in range(2):
                        src = psA[:, s * 512:(s + 1) * 512].rearrange("p (b t) -> p b t", b=4)
                        dst = pc4[:, s * 4:(s + 1) * 4, h, :]
                        ins = scalar.activation(out=dst, in_=src, func=AF.Identity, bias=bcol[:, oc:oc + 1])
                    P.multi_inc(scalar, ins, [("qkev", 1), ("wfree", 1)])
            else:
                h = oc - 32

                @block.scalar
                def _(scalar, oc=oc):
                    P.wait(scalar, "mmq")
                    P.wait(scalar, "vtpdone")
                    scalar.activation(out=xlnt[:, 0:512], in_=psA[:, 0:512],
                                      func=AF.Identity, bias=bcol[:, oc:oc + 1])
                    ins = scalar.activation(out=xlnt[:, 512:1024], in_=psA[:, 512:1024],
                                            func=AF.Identity, bias=bcol[:, oc:oc + 1])
                    P.multi_inc(scalar, ins, [("qkev", 1), ("vfm", 1)])

                for tt in range(NTT):
                    @block.tensor
                    def _(tensor, tt=tt):
                        P.wait(tensor, "vfm")
                        P.wait(tensor, "vte", lag=0)
                        ins = tensor.transpose(trP[:, 0:128], xlnt[:, tt * 128:(tt + 1) * 128], ident[:, :])
                        if tt == NTT - 1:
                            P.multi_inc(tensor, ins, [("vtp", 1), ("vtpdone", 1)])
                        else:
                            P.inc(ins, "vtp", 1)

                    @block.vector
                    def _(vector, tt=tt):
                        P.wait(vector, "vtp")
                        P.wait(vector, "vwr", lag=0)
                        ins = vector.tensor_copy(vstg[:, tt * 128:(tt + 1) * 128], trP[:, 0:128])
                        P.inc(ins, "vte", 1)

                @block.sync
                def _(sync, h=h):
                    P.wait(sync, "vte")
                    ins = sync.dma_start(out=v_loc[:, :, h * 128:(h + 1) * 128].rearrange("blk p t -> p blk t"),
                                         in_=vstg[:, :].rearrange("p (blk t) -> p blk t", blk=8))
                    P.inc(ins, "vwr", 16)
                    P.bump(sync, "wfree", 1)

        @block.sync
        def _(sync):
            P.wait(sync, "qkev")
            ins = sync.dma_start(out=k_loc[:, :, :].rearrange("blk p x -> p blk x"),
                                 in_=pC[:, :].rearrange("p (blk x) -> p blk x", blk=8))
            P.inc(ins, "kwr", 16)

        for src_name, src in (("q", pB), ("k", pC)):
            for ch in range(32):
                @block.scalar
                def _(scalar, src=src, ch=ch, src_name=src_name):
                    if src_name == "q" and ch == 0:
                        P.wait(scalar, "qkev")
                    P.wait(scalar, "nsq", lag=0)
                    ins = scalar.activation(out=sqs, in_=src[:, ch * 512:(ch + 1) * 512],
                                            func=AF.Square)
                    P.inc(ins, "sqr", 1)

                @block.tensor
                def _(tensor):
                    P.wait(tensor, "sqr")
                    P.wait(tensor, "nrd", lag=0)
                    ins = tensor.matmul(misc[0:1, :], onescol[:, 0:1], sqs, start=True, stop=True)
                    P.inc(ins, "nmm", 1)

                @block.vector
                def _(vector, src_name=src_name):
                    P.wait(vector, "nmm")
                    dst = cvals[:, 0:1] if src_name == "q" else cvals[:, 1:2]
                    vector.tensor_reduce(nrow[0:1, 0:1], misc[0:1, :], AX.X, ALU.max).then_inc(P.sem("nrd"), 1)
                    P.counts["nrd"] += 1
                    vector.wait_ge(P.sem("nrd"), P.counts["nrd"])
                    ins = vector.tensor_max(dst, dst, nrow[0:1, 0:1])
                    P.multi_inc(vector, ins, [("nmx", 1), ("nsq", 1)])

        @block.gpsimd
        def _(gpsimd):
            P.wait(gpsimd, "nmx")
            ins = gpsimd.memset(flag[:, :], MAGIC + 1)
            P.inc(ins, "flg", 1)

        @block.sync
        def _(sync):
            P.wait(sync, "flg")
            P.wait(sync, "kwr")
            P.wait(sync, "vwr")
            b = sync.bass
            engs = [sync.engine]
            regs = b.alloc_registers("parsw", engines=engs)
            pid = sync.partition_id()
            b.regs_mov(regs, pid % 2)
            parsv = b.snap(regs, engines=engs, min_val=0, max_val=1)
            pre = dict(P.counts)
            deltas = []
            for arm in sync.Switch(parsv, 2):
                ch = CHUNKS[arm]
                for sl in range(2):
                    c = ch[sl]
                    ins = sync.dma_start(out=k_hbm[c * 4:(c + 1) * 4, :, :], in_=k_loc[sl * 4:(sl + 1) * 4, :, :])
                    P.inc(ins, "xch", 16)
                    ins = sync.dma_start(out=v_hbm[c * 4:(c + 1) * 4, :, :], in_=v_loc[sl * 4:(sl + 1) * 4, :, :])
                    P.inc(ins, "xch", 16)
                ins = sync.dma_start(out=norm_sh[arm, :, :], in_=cvals[:, 0:4])
                P.inc(ins, "xch", 16)
                P.wait(sync, "xch")
                ins = sync.dma_start(out=flag_sh[arm:arm + 1, :].squeeze(0), in_=flag[0, :])
                P.inc(ins, "xch", 16)
                P.wait(sync, "xch")
                deltas.append({k: P.counts[k] - pre.get(k, 0) for k in P.counts})
                if arm == 0:
                    P.counts = dict(pre)
            assert deltas[0] == deltas[1], (deltas[0], deltas[1])
            P.counts = {k: pre.get(k, 0) + deltas[0].get(k, 0) for k in set(pre) | set(deltas[0])}

        @block.scalar
        def _(scalar):
            pid = scalar.partition_id()
            other = scalar.compute_val(1 - pid % 2)
            spin_poll(P, scalar, pflag, flag_sh[bass.ds(other, 1), :].squeeze(0), MAGIC + 1, "l0kv")
            ins = scalar.dma_start(out=pnorm[:, :], in_=norm_sh[bass.ds(other, 1), :, :].squeeze(0))
            P.inc(ins, "pnr", 16)
            P.wait(scalar, "pnr")
            P.bump(scalar, "pnd", 1)

        @block.vector
        def _(vector):
            P.wait(vector, "pnd")
            ins = vector.tensor_max(cvals[:, 1:2], cvals[:, 1:2], pnorm[:, 1:2])
            P.inc(ins, "cv1", 1)

        @block.scalar
        def _(scalar):
            P.wait(scalar, "cv1")
            scalar.activation(out=cvals[:, 2:3], in_=cvals[:, 0:1], func=AF.Ln).then_inc(P.sem("cv2"), 1)
            P.counts["cv2"] += 1
            scalar.wait_ge(P.sem("cv2"), P.counts["cv2"])
            scalar.activation(out=cvals[:, 3:4], in_=cvals[:, 1:2], func=AF.Ln).then_inc(P.sem("cv2"), 1)
            P.counts["cv2"] += 1

        @block.vector
        def _(vector):
            P.wait(vector, "cv2")
            ins = vector.tensor_add(cvals[:, 4:5], cvals[:, 2:3], cvals[:, 3:4])
            P.inc(ins, "cv3", 1)

        @block.scalar
        def _(scalar):
            P.wait(scalar, "cv3")
            ins = scalar.activation(out=cvals[:, 5:6], in_=cvals[:, 4:5], func=AF.Exp, scale=0.5)
            P.inc(ins, "cv4", 1)

        @block.vector
        def _(vector):
            P.wait(vector, "cv4")
            vector.tensor_scalar(out=cvals[:, 6:7], in0=cvals[:, 5:6], scalar1=SCALE0,
                                 scalar2=43.0, op0=ALU.mult, op1=ALU.min).then_inc(P.sem("cv5"), 1)
            P.counts["cv5"] += 1
            vector.wait_ge(P.sem("cv5"), P.counts["cv5"])
            ins = vector.tensor_scalar_mul(cvals[:, 7:8], cvals[:, 6:7], -1.0)
            P.inc(ins, "cv6", 1)

        @block.sync
        def _(sync):
            P.wait(sync, "cv6")
            ins = sync.dma_start(out=ncsc, in_=cvals[0:1, 7:8])
            P.inc(ins, "ncb0", 16)
            P.wait(sync, "ncb0")
            ins = sync.dma_start(out=negC[:, :], in_=ncsc.partition_broadcast(128))
            P.inc(ins, "ncb", 16)

        t0_d = (t0_s0, t0_s1)
        for s in range(2):
            for hg in range(H // 2):
                h0, h1 = 2 * hg, 2 * hg + 1
                njs = NJ[s]
                for jg in range(njs // 2):
                    jj = (2 * jg, 2 * jg + 1)

                    @block.sync
                    def _(sync, s=s, hg=hg, jg=jg, jj=jj, h0=h0):
                        if s == 0 and hg == 0 and jg == 0:
                            P.wait(sync, "ncb")
                        P.wait(sync, "kvfree")
                        for ji, j in enumerate(jj):
                            ins = sync.dma_start(out=kst[:, ji * 256:(ji + 1) * 256],
                                                 in_=k_hbm[j, :, h0 * 128:(h0 + 2) * 128])
                            P.inc(ins, "kld", 16)
                            ins = sync.dma_start(out=vst[:, ji * 256:(ji + 1) * 256],
                                                 in_=v_hbm[j, :, h0 * 128:(h0 + 2) * 128])
                            P.inc(ins, "kld", 16)

                    @block.gpsimd
                    def _(gpsimd, s=s, jj=jj, h0=h0, h1=h1):
                        P.wait(gpsimd, "Tfree")
                        for hi, h in enumerate((h0, h1)):
                            for ji, j in enumerate(jj):
                                ins = gpsimd.dma_start(
                                    out=Tst[:, (hi * 2 + ji) * 512:(hi * 2 + ji + 1) * 512],
                                    in_=t0_d[s][j, h, :, :])
                                P.inc(ins, "Tld", 16)

                    @block.tensor
                    def _(tensor, s=s, jj=jj, h0=h0, h1=h1):
                        P.wait(tensor, "kld")
                        P.wait(tensor, "sfree")
                        for hi, (h, ps) in enumerate(((h0, psA), (h1, psB))):
                            for ji, j in enumerate(jj):
                                ins = tensor.matmul(ps[:, ji * 512:(ji + 1) * 512],
                                                    kst[:, ji * 256 + hi * 128: ji * 256 + (hi + 1) * 128],
                                                    pB[:, h * TLOC + s * 512: h * TLOC + (s + 1) * 512],
                                                    start=True, stop=True)
                        P.inc(ins, "smm", 1)

                    @block.scalar
                    def _(scalar):
                        P.wait(scalar, "smm")
                        P.wait(scalar, "pb2f")
                        scalar.activation(out=pbuf[:, 0:1024], in_=psA[:, :], func=AF.Exp,
                                          bias=negC[:, 0:1], scale=SCALE0)
                        ins = scalar.activation(out=pbuf[:, 1024:2048], in_=psB[:, :], func=AF.Exp,
                                                bias=negC[:, 0:1], scale=SCALE0)
                        P.multi_inc(scalar, ins, [("exp", 1), ("sfree", 1)])

                    @block.gpsimd
                    def _(gpsimd, s=s, jj=jj, h0=h0, h1=h1):
                        P.wait(gpsimd, "exp")
                        P.wait(gpsimd, "Tld")
                        P.wait(gpsimd, "pb2f")
                        P.wait(gpsimd, "sumz")
                        ins = None
                        for hi in range(2):
                            for ji, j in enumerate(jj):
                                src = pbuf[:, hi * 1024 + ji * 512: hi * 1024 + (ji + 1) * 512]
                                tt_ = Tst[:, (hi * 2 + ji) * 512:(hi * 2 + ji + 1) * 512]
                                ins = gpsimd.tensor_tensor(src, src, tt_, ALU.mult)
                        P.inc(ins, "msk", 1)
                        gpsimd.wait_ge(P.sem("msk"), P.counts["msk"])
                        gpsimd.sem_inc(P.sem("Tfree"), 1)
                        P.counts["Tfree"] = P.counts.get("Tfree", 0) + 1
                        last = None
                        for hi, h in enumerate((h0, h1)):
                            for ji, j in enumerate(jj):
                                src = pbuf[:, hi * 1024 + ji * 512: hi * 1024 + (ji + 1) * 512]
                                gpsimd.tensor_reduce(nrow[0:1, :], src, AX.C, ALU.add).then_inc(P.sem("red"), 1)
                                P.counts["red"] += 1
                                gpsimd.wait_ge(P.sem("red"), P.counts["red"])
                                dsum = sums[0:1, hi * 512:(hi + 1) * 512]
                                last = gpsimd.tensor_tensor(dsum, dsum, nrow[0:1, :], ALU.add)
                                last.then_inc(P.sem("red"), 1)
                                P.counts["red"] += 1
                                gpsimd.wait_ge(P.sem("red"), P.counts["red"])
                        P.inc(gpsimd.nop(), "rdd", 1)

                    @block.tensor
                    def _(tensor, jj=jj, jg=jg, njs=njs):
                        P.wait(tensor, "rdd")
                        if jg == 0:
                            P.wait(tensor, "cev")
                        for hi in range(2):
                            for ji, j in enumerate(jj):
                                lhsT = vst[:, ji * 256 + hi * 128: ji * 256 + (hi + 1) * 128]
                                ins = tensor.matmul(ctxP[:, hi * 512:(hi + 1) * 512], lhsT,
                                                    pbuf[:, hi * 1024 + ji * 512: hi * 1024 + (ji + 1) * 512],
                                                    start=(jg == 0 and ji == 0),
                                                    stop=(jg == njs // 2 - 1 and ji == 1))
                        P.multi_inc(tensor, ins, [("cmm", 1), ("pb2f", 1), ("kvfree", 1)])

                if s == 0 and hg == 0 and "sums0" in dbg:
                    @block.gpsimd
                    def _(gpsimd):
                        P.wait(gpsimd, "cmm")
                        ins = gpsimd.dma_start(out=dbg["sums0"], in_=sums[0:1, :])
                        P.inc(ins, "dbgs", 16)
                for hi, h in enumerate((h0, h1)):
                    @block.vector
                    def _(vector, h=h, hi=hi):
                        P.wait(vector, "cmm")
                        P.wait(vector, "rte")
                        if s == 0 and hg == 0 and "sums0" in dbg:
                            P.wait(vector, "dbgs")
                        ins = vector.reciprocal(rsum[0:1, :], sums[0:1, hi * 512:(hi + 1) * 512])
                        P.inc(ins, "rcp", 1)

                    @block.tensor
                    def _(tensor):
                        P.wait(tensor, "rcp")
                        P.wait(tensor, "rtev", lag=0)
                        ins = tensor.matmul(misc[:, :], onesf[:, :], rsum[0:1, :], start=True, stop=True)
                        P.inc(ins, "rmm", 1)

                    @block.scalar
                    def _(scalar):
                        P.wait(scalar, "rmm")
                        ins = scalar.copy(rtile, misc[:, :])
                        P.inc(ins, "rte", 1)

                    @block.vector
                    def _(vector, s=s, h=h, hi=hi):
                        P.wait(vector, "rte")
                        dst = pC[:, h * TLOC + s * 512: h * TLOC + (s + 1) * 512]
                        vector.scalar_tensor_tensor(dst, ctxP[:, hi * 512:(hi + 1) * 512], 1.0,
                                                    rtile, op0=ALU.mult, op1=ALU.mult).then_inc(P.sem("cev"), 1)
                        P.counts["cev"] += 1
                        vector.wait_ge(P.sem("cev"), P.counts["cev"])
                        ins = vector.memset(sums[0:1, hi * 512:(hi + 1) * 512], 0.0)
                        P.multi_inc(vector, ins, [("rtev", 1), ("sumz", 1)])

        if "kstg" in dbg:
            @block.gpsimd
            def _(gpsimd):
                P.wait(gpsimd, "knev")
                ins = gpsimd.dma_start(out=dbg["kstg"], in_=pC[:, :])
                P.inc(ins, "dbgk2", 16)
        if "ckv" in dbg:
            @block.gpsimd
            def _(gpsimd):
                P.wait(gpsimd, "rmsd")
                ins = gpsimd.dma_start(out=dbg["ckv"], in_=ckv[:, :])
                P.inc(ins, "dbgc2", 16)
                gpsimd.wait_ge(P.sem("dbgc2"), P.counts["dbgc2"])
                P.wait(gpsimd, "kaev")
                ins = gpsimd.dma_start(out=dbg["krtm"], in_=krtm[:, :])
                P.inc(ins, "dbgc2", 16)
        if "kl0" in dbg:
            @block.gpsimd
            def _(gpsimd):
                P.wait(gpsimd, "kwr")
                ins = gpsimd.dma_start(out=dbg["kl0"], in_=k_loc[0, :, :])
                P.inc(ins, "dbgk", 16)
                gpsimd.wait_ge(P.sem("dbgk"), P.counts["dbgk"])
                if "vl0" in dbg:
                    P.wait(gpsimd, "vwr")
                    ins = gpsimd.dma_start(out=dbg["vl0"], in_=v_loc[0, :, :])
                    P.inc(ins, "dbgk", 16)
                    gpsimd.wait_ge(P.sem("dbgk"), P.counts["dbgk"])
                    P.wait(gpsimd, "krw")
                    ins = gpsimd.dma_start(out=dbg["kr0"], in_=kr_loc[0, :, :])
                    P.inc(ins, "dbgk", 16)
        if "qn0" in dbg:
            @block.gpsimd
            def _(gpsimd):
                P.wait(gpsimd, "qnw")
                ins = gpsimd.dma_start(out=dbg["qn0"], in_=qn_loc[0, :, :])
                P.inc(ins, "dbgn", 16)
                gpsimd.wait_ge(P.sem("dbgn"), P.counts["dbgn"])
                ins = gpsimd.dma_start(out=dbg["qr0"], in_=qr_loc[0, :, :])
                P.inc(ins, "dbgn", 16)
        if "cvals" in dbg:
            @block.gpsimd
            def _(gpsimd):
                P.wait(gpsimd, "ncb")
                ins = gpsimd.dma_start(out=dbg["cvals"], in_=cvals[:, :])
                P.inc(ins, "dbgv", 16)
        if "xfm" in dbg:
            @block.gpsimd
            def _(gpsimd):
                P.wait(gpsimd, "qkev")
                ins = gpsimd.dma_start(out=dbg["xfm"], in_=pA[:, :])
                P.inc(ins, "dbgx", 16)
        if "ctx" in dbg:
            @block.gpsimd
            def _(gpsimd):
                P.wait(gpsimd, "cev")
                ins = gpsimd.dma_start(out=dbg["ctx"], in_=pC[:, :])
                P.inc(ins, "dbgc", 16)
        if "qfm" in dbg:
            @block.gpsimd
            def _(gpsimd):
                P.wait(gpsimd, "qkev")
                ins = gpsimd.dma_start(out=dbg["qfm"], in_=pB[:, :])
                P.inc(ins, "dbgq", 16)
        if "kst" in dbg:
            @block.gpsimd
            def _(gpsimd):
                P.wait(gpsimd, "qkev")
                ins = gpsimd.dma_start(out=dbg["kst"], in_=pC[:, :])
                P.inc(ins, "dbgk", 16)

        for og in range(4):
            for sub in range(4):
                oc = og * 4 + sub

                @block.sync
                def _(sync, oc=oc, og=og, sub=sub):
                    if oc == 0:
                        P.wait(sync, "cev")
                        if "qfm" in dbg:
                            P.wait(sync, "dbgq")
                    if sub == 0 and og > 0:
                        P.wait(sync, "oev")
                    P.wait(sync, "wocv")
                    ins = sync.dma_start(out=pD[:, :], in_=wout[oc * 128:(oc + 1) * 128, :])
                    P.inc(ins, "wold", 16)

                @block.vector
                def _(vector):
                    P.wait(vector, "wold")
                    P.wait(vector, "woev")
                    ins = vector.tensor_copy(pE[:, 0:D], pD[:, :])
                    P.inc(ins, "wocv", 1)

                for cg in range(4):
                    @block.tensor
                    def _(tensor, cg=cg):
                        P.wait(tensor, "wocv")
                        P.wait(tensor, "woev")
                        for ci in range(4):
                            c = cg * 4 + ci
                            ins = tensor.transpose(trP[:, ci * 128:(ci + 1) * 128],
                                                   pE[:, c * 128:(c + 1) * 128], ident[:, :])
                        P.inc(ins, "wotp", 1)

                    @block.scalar
                    def _(scalar, cg=cg, sub=sub):
                        P.wait(scalar, "wotp")
                        for ci in range(4):
                            c = cg * 4 + ci
                            ins = scalar.copy(pA[:, c * 512 + sub * 128: c * 512 + (sub + 1) * 128],
                                              trP[:, ci * 128:(ci + 1) * 128])
                        P.inc(ins, "woev", 1)

            for tt in range(NTT):
                s_, tl = tt // 4, tt % 4

                @block.tensor
                def _(tensor, og=og, s_=s_, tl=tl):
                    P.wait(tensor, "woev")
                    P.wait(tensor, "oev", lag=0)
                    for c in range(NC):
                        lhsT = pC[:, c * TLOC + s_ * 512 + tl * 128: c * TLOC + s_ * 512 + (tl + 1) * 128]
                        tensor.matmul(misc[:, :], lhsT, pA[:, c * 512:(c + 1) * 512],
                                      start=(c == 0), stop=False)
                    ins = tensor.matmul(misc[:, :], onesbr[0:1, :],
                                        borowb[0:1, og * 512:(og + 1) * 512], start=False, stop=True)
                    P.inc(ins, "omm", 1)

                @block.scalar
                def _(scalar, og=og, tt=tt):
                    P.wait(scalar, "omm")
                    ins = scalar.copy(pB[:, tt * 2048 + og * 512: tt * 2048 + (og + 1) * 512], misc[:, :])
                    P.inc(ins, "oev", 1)

        for tt in range(NTT):
            at = pB[:, tt * 2048:(tt + 1) * 2048]
            st = stats[:, tt * 32:tt * 32 + 24]
            ag = stats[:, tt * 32 + 24:tt * 32 + 26]
            emit_ln_stats(P, block, at, st, ag, LN_EPS, "ep",
                          pre_wait=["oev"] if tt == 0 else ["epd"])

            @block.vector
            def _(vector, at=at, ag=ag, tt=tt):
                P.wait(vector, "eps")
                vector.tensor_scalar(out=pD[:, :], in0=at, scalar1=ag[:, 0:1],
                                     scalar2=ag[:, 1:2], op0=ALU.subtract, op1=ALU.mult).then_inc(P.sem("epn"), 1)
                P.counts["epn"] += 1
                vector.wait_ge(P.sem("epn"), P.counts["epn"])
                ins = vector.tensor_tensor(x_tm[:, tt * D:(tt + 1) * D],
                                           x_tm[:, tt * D:(tt + 1) * D], pD[:, :], ALU.add)
                P.inc(ins, "epd", 1)

            @block.sync
            def _(sync, tt=tt):
                P.wait(sync, "epd")
                ins = sync.dma_start(out=x_out.rearrange("(tt p) d -> p tt d", p=128)[:, tt, :],
                                     in_=x_tm[:, tt * D:(tt + 1) * D])
                P.inc(ins, "xout", 16)

        @block.sync
        def _(sync):
            P.wait(sync, "xout")
            for name in dbg:
                P.wait(sync, "dbg" + name[0])
            P.bump(sync, "alldone", 1)

    return nc



def shard_rows(full, parity):
    c = CHUNKS[parity]
    return np.concatenate([full[c[0] * 512:(c[0] + 1) * 512], full[c[1] * 512:(c[1] + 1) * 512]], axis=0)


def unshard_rows(parts):
    out = []
    for b in range(B):
        rows = np.empty((S,) + parts[0].shape[1:], dtype=parts[0].dtype)
        p0, p1 = parts[2 * b], parts[2 * b + 1]
        rows[0:512] = p0[0:512]
        rows[1536:2048] = p0[512:1024]
        rows[512:1024] = p1[0:512]
        rows[1024:1536] = p1[512:1024]
        out.append(rows)
    return np.stack(out)


_L0A_CACHE = {}


def run_l0_attn(x, Wqkv, b_qkv, Wout, b_out, debug=()):
    key = tuple(d[0] for d in debug)
    if key not in _L0A_CACHE:
        _L0A_CACHE[key] = build_l0_attn(debug)
    nc = _L0A_CACHE[key]
    in_maps = []
    for core in range(8):
        b, par = core // 2, core % 2
        cst = host_constants(par)
        in_maps.append({
            "x_in": np.ascontiguousarray(shard_rows(x[b], par)),
            "wqkv": Wqkv, "bqkv": np.ascontiguousarray(b_qkv.reshape(48, 128).T),
            "wout": Wout, "bout": b_out.reshape(1, -1),
            "t0_s0": cst["t0_s0"], "t0_s1": cst["t0_s1"],
            "ident": cst["ident"],
        })
    res = run_bass_kernel_spmd(nc, in_maps, core_ids=list(range(8)))
    x1 = unshard_rows([res.results[c]["x_out"] for c in range(8)])
    return x1, res.results



def build_l1_ffn(debug=()):
    nc = bass.Bass("TRN2", target_bir_lowering=False, debug=False)
    x_in = nc.dram_tensor("x_in", [TLOC, D], F32, kind="ExternalInput").ap()
    facts = nc.dram_tensor("facts", [8, D, D], F32, kind="ExternalInput").ap()
    fbias = nc.dram_tensor("fbias", [1, D], F32, kind="ExternalInput").ap()
    ident_d = nc.dram_tensor("ident", [128, 128], BF16, kind="ExternalInput").ap()
    x_out = nc.dram_tensor("x_out", [TLOC, D], F32, kind="ExternalOutput").ap()
    dbg = {}
    for name, shp, dt_ in debug:
        dbg[name] = nc.dram_tensor("dbg_" + name, shp, dt_, kind="ExternalOutput").ap()

    with ExitStack() as ctx:
        E = ctx.enter_context
        P = Prog(nc, ctx)
        x_tm = E(nc.sbuf_tensor("x_tm", [128, NTT * D], F32))
        pA = E(nc.sbuf_tensor("pA", [128, NC * TLOC], BF16))
        F = E(nc.sbuf_tensor("F", [128, NC * D], BF16))
        pD = E(nc.sbuf_tensor("pD", [128, D], F32))
        ob = E(nc.sbuf_tensor("ob", [128, 2 * D], F32))
        xlnt = E(nc.sbuf_tensor("xlnt", [128, D], BF16))
        stats = E(nc.sbuf_tensor("stats", [128, 256], F32))
        stats2 = E(nc.sbuf_tensor("stats2", [128, 256], F32))
        ident = E(nc.sbuf_tensor("identt", [128, 128], BF16))
        onesf = E(nc.sbuf_tensor("onesft", [1, 128], F32))
        brow = E(nc.sbuf_tensor("brow", [1, D], F32))
        brow2 = E(nc.sbuf_tensor("brow2", [1, D], F32))
        psA = E(nc.psum_tensor("psA", [128, 1024], F32))
        misc = E(nc.psum_tensor("misc", [128, 512], F32))
        trP = E(nc.psum_tensor("trP", [128, 512], BF16))
        block = E(nc.Block())

        @block.sync
        def _(sync):
            ins = sync.dma_start(out=x_tm[:, :].rearrange("p (tt d) -> p tt d", d=D),
                                 in_=x_in.rearrange("(tt p) d -> p tt d", p=128))
            P.inc(ins, "ld", 16)
            ins = sync.dma_start(out=ident[:, :], in_=ident_d)
            P.inc(ins, "ld", 16)
            ins = sync.dma_start(out=brow[:, :], in_=fbias)
            P.inc(ins, "ld", 16)

        @block.vector
        def _(vector):
            P.wait(vector, "ld")
            vector.memset(onesf[:, :], 1.0)
            ins = vector.tensor_scalar_mul(brow2[:, :], brow[:, :], float(math.sqrt(8.0)))
            P.inc(ins, "cst", 1)

        emit_ln_and_fm(P, block, x_tm, xlnt, stats, trP, ident, pA, LN_EPS,
                       tag="l1f", extra_wait="ld")

        for c in range(NC):
            @block.sync
            def _(sync, c=c):
                P.wait(sync, "fcv")
                if c == 0:
                    P.wait(sync, "cst")
                ins = sync.dma_start(out=pD[:, :], in_=facts[0, c * 128:(c + 1) * 128, :])
                P.inc(ins, "fl0", 16)

            @block.gpsimd
            def _(gpsimd, c=c):
                P.wait(gpsimd, "fl0")
                for r in range(1, 8):
                    ins = gpsimd.dma_start(out=pD[:, :], in_=facts[r, c * 128:(c + 1) * 128, :],
                                           accum_op=ALU.add)
                    P.inc(ins, "fac", 16)
                    gpsimd.wait_ge(P.sem("fac"), P.counts["fac"])

            @block.vector
            def _(vector, c=c):
                P.wait(vector, "fac")
                ins = vector.tensor_copy(F[:, c * D:(c + 1) * D], pD[:, :])
                P.inc(ins, "fcv", 1)

        for tt in range(NTT):
            for og in range(4):
                @block.tensor
                def _(tensor, tt=tt, og=og):
                    P.wait(tensor, "fcv")
                    P.wait(tensor, "l1ffm")
                    P.wait(tensor, "oev", lag=0)
                    for c in range(NC):
                        lhsT = pA[:, c * TLOC + tt * 128:c * TLOC + (tt + 1) * 128]
                        tensor.matmul(misc[:, :], lhsT, F[:, c * D + og * 512:c * D + (og + 1) * 512],
                                      start=(c == 0), stop=False)
                    ins = tensor.matmul(misc[:, :], onesf[:, :], brow2[0:1, og * 512:(og + 1) * 512],
                                        start=False, stop=True)
                    P.inc(ins, "omm", 1)

                @block.scalar
                def _(scalar, tt=tt, og=og):
                    P.wait(scalar, "omm")
                    P.wait(scalar, "epd1", lag=1)
                    ins = scalar.activation(out=ob[:, (tt % 2) * D + og * 512:(tt % 2) * D + (og + 1) * 512],
                                            in_=misc[:, :], func=AF.Copy, scale=float(1.0 / math.sqrt(8.0)))
                    P.inc(ins, "oev", 1)

            at = ob[:, (tt % 2) * D:(tt % 2 + 1) * D]
            st = stats2[:, tt * 32:tt * 32 + 24]
            ag = stats2[:, tt * 32 + 24:tt * 32 + 26]
            emit_ln_stats(P, block, at, st, ag, LN_EPS, "ep1", pre_wait=["oev"])

            @block.vector
            def _(vector, at=at, ag=ag, tt=tt):
                P.wait(vector, "ep1s")
                vector.tensor_scalar(out=at, in0=at, scalar1=ag[:, 0:1],
                                     scalar2=ag[:, 1:2], op0=ALU.subtract, op1=ALU.mult).then_inc(P.sem("epn1"), 1)
                P.counts["epn1"] += 1
                vector.wait_ge(P.sem("epn1"), P.counts["epn1"])
                ins = vector.tensor_tensor(x_tm[:, tt * D:(tt + 1) * D],
                                           x_tm[:, tt * D:(tt + 1) * D], at, ALU.add)
                P.inc(ins, "epd1", 1)

            @block.sync
            def _(sync, tt=tt):
                P.wait(sync, "epd1")
                ins = sync.dma_start(out=x_out.rearrange("(tt p) d -> p tt d", p=128)[:, tt, :],
                                     in_=x_tm[:, tt * D:(tt + 1) * D])
                P.inc(ins, "xout", 16)

        @block.sync
        def _(sync):
            P.wait(sync, "xout")
            P.bump(sync, "alldone", 1)
    return nc


def run_l1_ffn(x, factors, fbias):
    if "l1ffn" not in _L0A_CACHE:
        _L0A_CACHE["l1ffn"] = build_l1_ffn()
    nc = _L0A_CACHE["l1ffn"]
    in_maps = []
    for core in range(8):
        b, par = core // 2, core % 2
        cst = host_constants(par)
        in_maps.append({
            "x_in": np.ascontiguousarray(shard_rows(x[b], par)),
            "facts": factors, "fbias": fbias.reshape(1, -1),
            "ident": cst["ident"],
        })
    res = run_bass_kernel_spmd(nc, in_maps, core_ids=list(range(8)))
    return unshard_rows([res.results[c]["x_out"] for c in range(8)]), res.results



def emit_row_ln(P, block, row, st, ag, eps, tag, gamma=None, beta=None, out=None,
                pre_wait=None):
    n = row.shape[1]
    nch = (n + 511) // 512
    out = row if out is None else out

    @block.vector
    def _(vector):
        if pre_wait:
            for w in pre_wait:
                P.wait(vector, w)
        for i in range(nch):
            ins = vector.bn_stats(st[0:1, i * 6:(i + 1) * 6], row[0:1, i * 512:(i + 1) * 512])
        ins.then_inc(P.sem(tag + "v"), 1)
        P.counts[tag + "v"] += 1
        vector.wait_ge(P.sem(tag + "v"), P.counts[tag + "v"])
        vector.bn_aggr(ag[0:1, 0:2], st[0:1, 0:6 * nch]).then_inc(P.sem(tag + "v"), 1)
        P.counts[tag + "v"] += 1
        vector.wait_ge(P.sem(tag + "v"), P.counts[tag + "v"])
        vector.tensor_scalar_add(ag[0:1, 1:2], ag[0:1, 1:2], float(eps)).then_inc(P.sem(tag + "v"), 1)
        P.counts[tag + "v"] += 1

    emit_rstd(P, block, ag[0:1, :], eps, tag)

    @block.vector
    def _(vector):
        P.wait(vector, tag + "s")
        vector.tensor_scalar(out=out[0:1, :], in0=row[0:1, :], scalar1=ag[0:1, 0:1],
                             scalar2=ag[0:1, 1:2], op0=ALU.subtract, op1=ALU.mult).then_inc(P.sem(tag + "a"), 1)
        P.counts[tag + "a"] += 1
        if gamma is not None:
            vector.wait_ge(P.sem(tag + "a"), P.counts[tag + "a"])
            vector.tensor_tensor(out[0:1, :], out[0:1, :], gamma[0:1, :], ALU.mult).then_inc(P.sem(tag + "a"), 1)
            P.counts[tag + "a"] += 1
            vector.wait_ge(P.sem(tag + "a"), P.counts[tag + "a"])
            vector.tensor_tensor(out[0:1, :], out[0:1, :], beta[0:1, :], ALU.add).then_inc(P.sem(tag + "a"), 1)
            P.counts[tag + "a"] += 1


def build_l0_ffn_a():
    nc = bass.Bass("TRN2", target_bir_lowering=False, debug=False)
    xlast = nc.dram_tensor("xlast", [1, D], F32, kind="ExternalInput").ap()
    in_g = nc.dram_tensor("in_g", [1, D], F32, kind="ExternalInput").ap()
    in_b = nc.dram_tensor("in_b", [1, D], F32, kind="ExternalInput").ap()
    aggw = nc.dram_tensor("aggw", [D, D], F32, kind="ExternalInput").ap()
    aggb = nc.dram_tensor("aggb", [1, D], F32, kind="ExternalInput").ap()
    png = nc.dram_tensor("png", [128, NC], F32, kind="ExternalInput").ap()
    pnb = nc.dram_tensor("pnb", [128, NC], F32, kind="ExternalInput").ap()
    d1w = nc.dram_tensor("d1w", [D, 4096], F32, kind="ExternalInput").ap()
    d1b = nc.dram_tensor("d1b", [1, 4096], F32, kind="ExternalInput").ap()
    d2w = nc.dram_tensor("d2w", [4096, D], F32, kind="ExternalInput").ap()
    d2b = nc.dram_tensor("d2b", [1, D], F32, kind="ExternalInput").ap()
    bcw = nc.dram_tensor("bcw", [D, D], F32, kind="ExternalInput").ap()
    bcb = nc.dram_tensor("bcb", [1, D], F32, kind="ExternalInput").ap()
    bc_out = nc.dram_tensor("bc_out", [1, D], F32, kind="ExternalOutput").ap()

    part_sh = nc.dram_tensor("part_sh", [2, 1, D], F32, kind="Internal", addr_space="Shared").ap()
    flag_sh = nc.dram_tensor("flag_sh", [2, 1], I32, kind="Internal", addr_space="Shared").ap()

    with ExitStack() as ctx:
        E = ctx.enter_context
        P = Prog(nc, ctx)
        xr = E(nc.sbuf_tensor("xr", [1, D], F32))
        ing = E(nc.sbuf_tensor("ing", [1, D], F32))
        inb = E(nc.sbuf_tensor("inb", [1, D], F32))
        xnlb = E(nc.sbuf_tensor("xnlb", [1, D], BF16))
        xnl_fm = E(nc.sbuf_tensor("xnl_fm", [128, NC], BF16))
        th_fm = E(nc.sbuf_tensor("th_fm", [128, NC], F32))
        thb = E(nc.sbuf_tensor("thb", [128, NC], BF16))
        tact = E(nc.sbuf_tensor("tact", [1, 4096], BF16))
        tact_fm = E(nc.sbuf_tensor("tact_fm", [128, 32], BF16))
        part = E(nc.sbuf_tensor("part", [1, D], F32))
        ppart = E(nc.sbuf_tensor("ppart", [1, D], F32))
        d2br = E(nc.sbuf_tensor("d2br", [1, D], F32))
        d1br = E(nc.sbuf_tensor("d1br", [1, 4096], BF16))
        aggbr = E(nc.sbuf_tensor("aggbr", [1, D], BF16))
        bcbr = E(nc.sbuf_tensor("bcbr", [1, D], BF16))
        pngs = E(nc.sbuf_tensor("pngs", [128, NC], F32))
        pnbs = E(nc.sbuf_tensor("pnbs", [128, NC], F32))
        wstr = E(nc.sbuf_tensor("wstr", [128, 32 * 512], BF16))
        ones1b = E(nc.sbuf_tensor("ones1b", [1, 1], BF16))
        ones1f = E(nc.sbuf_tensor("ones1f", [1, 1], F32))
        onesr = E(nc.sbuf_tensor("onesr", [1, 128], F32))
        onesc = E(nc.sbuf_tensor("onesc", [128, 1], F32))
        stats = E(nc.sbuf_tensor("stats", [1, 32], F32))
        sc = E(nc.sbuf_tensor("sc", [1, 16], F32))
        mcol = E(nc.sbuf_tensor("mcol", [128, 2], F32))
        sqf = E(nc.sbuf_tensor("sqf", [128, NC], F32))
        flag = E(nc.sbuf_tensor("flag", [1, 1], I32))
        pflag = E(nc.sbuf_tensor("pflag", [1, 1], I32))
        rowp = E(nc.psum_tensor("rowp", [1, 512], F32))
        colp = E(nc.psum_tensor("colp", [128, 32], F32))
        sump = E(nc.psum_tensor("sump", [1, 32], F32))
        block = E(nc.Block())

        @block.sync
        def _(sync):
            for t, d_ in ((xr, xlast), (ing, in_g), (inb, in_b), (part, d2b),
                          (pngs, png), (pnbs, pnb)):
                ins = sync.dma_start(out=t[:, :], in_=d_)
                P.inc(ins, "ld", 16)

        @block.gpsimd
        def _(gpsimd):
            for t, d_ in ((d1br, d1b), (aggbr, aggb), (bcbr, bcb)):
                ins = gpsimd.dma_start(out=t[:, :], in_=d_)
                P.inc(ins, "ld", 16)

        @block.vector
        def _(vector):
            P.wait(vector, "ld")
            vector.memset(ones1b[:, :], 1.0)
            vector.memset(ones1f[:, :], 1.0)
            vector.memset(onesr[:, :], 1.0)
            vector.memset(onesc[:, :], 1.0)
            ins = vector.tensor_copy(d2br[:, :], part[0:1, :])
            P.inc(ins, "cst", 1)

        parh = {}

        @block.scalar
        def _(scalar):
            pid = scalar.partition_id()
            parh["par"] = scalar.compute_val(pid % 2)
            parh["other"] = scalar.compute_val(1 - pid % 2)

        emit_row_ln(P, block, xr, stats, sc[0:1, 0:2], FFN_EPS, "xl",
                    gamma=ing, beta=inb, out=xr, pre_wait=["cst"])

        @block.vector
        def _(vector):
            P.wait(vector, "xla")
            ins = vector.tensor_copy(xnlb[:, :], xr[0:1, :])
            P.inc(ins, "xlb", 1)

        def k1_transpose(src_row, dst_fm, nchunk, tag, dtype_f32=False):
            @block.tensor
            def _(tensor):
                P.wait(tensor, tag + "w")
                for c in range(nchunk):
                    ins = tensor.matmul(colp[:, c:c + 1], src_row[0:1, c * 128:(c + 1) * 128],
                                        ones1f[:, :] if dtype_f32 else ones1b[:, :],
                                        start=True, stop=True, skip_group_check=True)
                P.inc(ins, tag + "t", 1)

            @block.vector
            def _(vector):
                P.wait(vector, tag + "t")
                ins = vector.tensor_copy(dst_fm[:, 0:nchunk], colp[:, 0:nchunk])
                P.inc(ins, tag + "e", 1)

        @block.scalar
        def _(scalar):
            P.wait(scalar, "xlb")
            P.bump(scalar, "x1w", 1)
        k1_transpose(xnlb, xnl_fm, NC, "x1")

        seqstate = {"prev_mm": None, "prev_ev": None}

        def fm_matvec(tag, lhs_fm, nchunk, wsrc, wcols, bias_row, out_cb, ev_name):
            nog = wcols // 512
            pm, pe = seqstate["prev_mm"], seqstate["prev_ev"]
            for og in range(nog):
                @block.gpsimd
                def _(gpsimd, og=og):
                    if og == 0 and pm is not None:
                        P.wait(gpsimd, pm)
                    P.wait(gpsimd, tag + "mm", lag=0)
                    ins = gpsimd.dma_start(
                        out=wstr[:, 0:nchunk * 512].rearrange("p (c x) -> p c x", x=512),
                        in_=wsrc[:, og * 512:(og + 1) * 512].rearrange("(c p) x -> p c x", p=128))
                    P.inc(ins, tag + "wl", 16)

                @block.tensor
                def _(tensor, og=og):
                    P.wait(tensor, tag + "wl")
                    P.wait(tensor, tag + "lhs")
                    if og == 0 and pe is not None:
                        P.wait(tensor, pe)
                    P.wait(tensor, ev_name, lag=0)
                    last = None
                    for c in range(nchunk):
                        last = tensor.matmul(rowp[0:1, :], lhs_fm[:, c:c + 1],
                                             wstr[:, c * 512:(c + 1) * 512],
                                             start=(c == 0), stop=(bias_row is None and c == nchunk - 1))
                    if bias_row is not None:
                        last = tensor.matmul(rowp[0:1, :], ones1b[:, :],
                                             bias_row[0:1, og * 512:(og + 1) * 512],
                                             start=False, stop=True)
                    P.inc(last, tag + "mm", 1)

                out_cb(og)
            seqstate["prev_mm"] = tag + "mm"
            seqstate["prev_ev"] = ev_name

        @block.scalar
        def _(scalar):
            P.wait(scalar, "x1e")
            P.bump(scalar, "agglhs", 1)

        def agg_out(og):
            @block.scalar
            def _(scalar, og=og):
                P.wait(scalar, "aggmm")
                ins = scalar.copy(part[0:1, og * 512:(og + 1) * 512], rowp[0:1, :])
                P.inc(ins, "aggev", 1)
        fm_matvec("agg", xnl_fm, NC, aggw, D, aggbr, agg_out, "aggev")

        @block.scalar
        def _(scalar):
            P.wait(scalar, "aggev")
            P.bump(scalar, "th0w", 1)
        k1_transpose(part, th_fm, NC, "th0", dtype_f32=True)

        for it in range(4):
            tg = "it"
            tgu = f"i{it}"
            @block.scalar
            def _(scalar, tg=tg):
                P.wait(scalar, "th0e" if it == 0 else "itupe")
                ins = scalar.activation(out=sqf[:, :], in_=th_fm[:, :], func=AF.Square)
                P.inc(ins, tg + "sq", 1)

            @block.tensor
            def _(tensor, tg=tg):
                P.wait(tensor, tg + "sq")
                tensor.matmul(sump[0:1, 0:16], onesc[:, 0:1], th_fm[:, :], start=True, stop=True,
                              skip_group_check=True)
                ins = tensor.matmul(sump[0:1, 16:32], onesc[:, 0:1], sqf[:, :], start=True, stop=True,
                                    skip_group_check=True)
                P.inc(ins, tg + "smm", 1)

            @block.vector
            def _(vector, tg=tg):
                P.wait(vector, tg + "smm")
                vector.tensor_reduce(sc[0:1, 0:1], sump[0:1, 0:16], AX.X, ALU.add).then_inc(P.sem(tg + "r"), 1)
                P.counts[tg + "r"] += 1
                vector.wait_ge(P.sem(tg + "r"), P.counts[tg + "r"])
                vector.tensor_reduce(sc[0:1, 1:2], sump[0:1, 16:32], AX.X, ALU.add).then_inc(P.sem(tg + "r"), 1)
                P.counts[tg + "r"] += 1
                vector.wait_ge(P.sem(tg + "r"), P.counts[tg + "r"])
                vector.tensor_scalar_mul(sc[0:1, 2:3], sc[0:1, 0:1], 1.0 / D).then_inc(P.sem(tg + "r"), 1)
                P.counts[tg + "r"] += 1
                vector.wait_ge(P.sem(tg + "r"), P.counts[tg + "r"])
                vector.tensor_scalar_mul(sc[0:1, 3:4], sc[0:1, 1:2], 1.0 / D).then_inc(P.sem(tg + "r"), 1)
                P.counts[tg + "r"] += 1
                vector.wait_ge(P.sem(tg + "r"), P.counts[tg + "r"])
                vector.tensor_tensor(sc[0:1, 4:5], sc[0:1, 2:3], sc[0:1, 2:3], ALU.mult).then_inc(P.sem(tg + "r"), 1)
                P.counts[tg + "r"] += 1
                vector.wait_ge(P.sem(tg + "r"), P.counts[tg + "r"])
                vector.tensor_sub(sc[0:1, 5:6], sc[0:1, 3:4], sc[0:1, 4:5]).then_inc(P.sem(tg + "r"), 1)
                P.counts[tg + "r"] += 1
                vector.wait_ge(P.sem(tg + "r"), P.counts[tg + "r"])
                ins = vector.tensor_scalar_add(sc[0:1, 5:6], sc[0:1, 5:6], float(FFN_EPS))
                P.inc(ins, tg + "var", 1)

            @block.scalar
            def _(scalar, tg=tg):
                P.wait(scalar, tg + "var")
                scalar.activation(out=sc[0:1, 6:7], in_=sc[0:1, 5:6], func=AF.Ln).then_inc(P.sem(tg + "ln"), 1)
                P.counts[tg + "ln"] += 1
                scalar.wait_ge(P.sem(tg + "ln"), P.counts[tg + "ln"])
                ins = scalar.activation(out=sc[0:1, 7:8], in_=sc[0:1, 6:7], func=AF.Exp, scale=-0.5)
                P.inc(ins, tg + "rs", 1)

            @block.tensor
            def _(tensor, tg=tg):
                P.wait(tensor, tg + "rs")
                tensor.matmul(colp[:, 0:1], onesr[0:1, :], sc[0:1, 2:3], start=True, stop=True,
                              skip_group_check=True)
                ins = tensor.matmul(colp[:, 1:2], onesr[0:1, :], sc[0:1, 7:8], start=True, stop=True,
                                    skip_group_check=True)
                P.inc(ins, tg + "bc", 1)

            @block.vector
            def _(vector, tg=tg):
                P.wait(vector, tg + "bc")
                vector.tensor_copy(mcol[:, 0:2], colp[:, 0:2]).then_inc(P.sem(tg + "mc"), 1)
                P.counts[tg + "mc"] += 1
                vector.wait_ge(P.sem(tg + "mc"), P.counts[tg + "mc"])
                vector.tensor_scalar(out=sqf[:, :], in0=th_fm[:, :], scalar1=mcol[:, 0:1],
                                     scalar2=mcol[:, 1:2], op0=ALU.subtract, op1=ALU.mult).then_inc(P.sem(tg + "mc"), 1)
                P.counts[tg + "mc"] += 1
                vector.wait_ge(P.sem(tg + "mc"), P.counts[tg + "mc"])
                vector.scalar_tensor_tensor(sqf[:, :], sqf[:, :], 1.0, pngs[:, :],
                                            op0=ALU.mult, op1=ALU.mult).then_inc(P.sem(tg + "mc"), 1)
                P.counts[tg + "mc"] += 1
                vector.wait_ge(P.sem(tg + "mc"), P.counts[tg + "mc"])
                ins = vector.tensor_tensor(thb[:, :], sqf[:, :], pnbs[:, :], ALU.add)
                P.inc(ins, tg + "lhs", 1)

            def d1_out(og, tg=tg):
                @block.scalar
                def _(scalar, og=og):
                    P.wait(scalar, tg + "mm")
                    ins = scalar.activation(out=tact[0:1, og * 512:(og + 1) * 512], in_=rowp[0:1, :],
                                            func=AF.Gelu)
                    P.inc(ins, tg + "ev", 1)
            fm_matvec(tg, thb, NC, d1w, 4096, d1br, d1_out, tg + "ev")

            @block.scalar
            def _(scalar, tg=tg):
                P.wait(scalar, tg + "ev")
                P.bump(scalar, tg + "tw", 1)
            k1_transpose(tact, tact_fm, 32, tg + "t")

            def d2_out(og, tg=tg):
                @block.scalar
                def _(scalar, og=og):
                    P.wait(scalar, tg + "2mm")
                    ins = scalar.copy(part[0:1, og * 512:(og + 1) * 512], rowp[0:1, :])
                    P.inc(ins, tg + "2ev", 1)

            @block.scalar
            def _(scalar, tg=tg):
                P.wait(scalar, tg + "te")
                P.bump(scalar, tg + "2lhs", 1)
            fm_matvec(tg + "2", tact_fm, 32, d2w, D, None, d2_out, tg + "2ev")

            @block.gpsimd
            def _(gpsimd, it=it):
                P.wait(gpsimd, tg + "2ev")
                ins = gpsimd.memset(flag[:, :], MAGIC + 2 + it)
                P.inc(ins, tg + "fl", 1)

            @block.scalar
            def _(scalar, tg=tg, it=it):
                P.wait(scalar, tg + "fl")
                par = parh["par"]
                other = parh["other"]
                ins = scalar.dma_start(out=part_sh[bass.ds(par, 1), :, :].squeeze(0), in_=part[0:1, :])
                P.inc(ins, tg + "sh", 16)
                P.wait(scalar, tg + "sh")
                ins = scalar.dma_start(out=flag_sh[bass.ds(par, 1), :].squeeze(0), in_=flag[0, :])
                P.inc(ins, tg + "sh", 16)
                P.wait(scalar, tg + "sh")
                spin_poll(P, scalar, pflag, flag_sh[bass.ds(other, 1), :].squeeze(0), MAGIC + 2 + it, tgu, sem_name="poll_it")
                ins = scalar.dma_start(out=ppart[0:1, :], in_=part_sh[bass.ds(other, 1), :, :].squeeze(0))
                P.inc(ins, tg + "pp", 16)
                P.wait(scalar, tg + "pp")
                P.bump(scalar, tg + "ppd", 1)

            @block.vector
            def _(vector, tg=tg):
                P.wait(vector, tg + "ppd")
                vector.tensor_add(part[0:1, :], part[0:1, :], ppart[0:1, :]).then_inc(P.sem(tg + "dd"), 1)
                P.counts[tg + "dd"] += 1
                vector.wait_ge(P.sem(tg + "dd"), P.counts[tg + "dd"])
                ins = vector.tensor_add(part[0:1, :], part[0:1, :], d2br[0:1, :])
                P.inc(ins, tg + "dw", 1)
            k1_transpose(part, sqf, NC, tg + "d", dtype_f32=True)

            @block.vector
            def _(vector, tg=tg):
                P.wait(vector, tg + "de")
                ins = vector.tensor_tensor(th_fm[:, :], th_fm[:, :], sqf[:, 0:NC], ALU.add)
                P.inc(ins, tg + "upe", 1)

        @block.vector
        def _(vector):
            P.wait(vector, "itupe")
            ins = vector.tensor_copy(thb[:, :], th_fm[:, :])
            P.inc(ins, "bclhs", 1)

        def bc_cb(og):
            @block.scalar
            def _(scalar, og=og):
                P.wait(scalar, "bcmm")
                ins = scalar.copy(part[0:1, og * 512:(og + 1) * 512], rowp[0:1, :])
                P.inc(ins, "bcev", 1)
        fm_matvec("bc", thb, NC, bcw, D, bcbr, bc_cb, "bcev")

        @block.sync
        def _(sync):
            P.wait(sync, "bcev")
            ins = sync.dma_start(out=bc_out, in_=part[0:1, :])
            P.inc(ins, "bco", 16)
            P.wait(sync, "bco")
    return nc


def run_l0_ffn_a(x1, inp):
    if "l0fa" not in _L0A_CACHE:
        _L0A_CACHE["l0fa"] = build_l0_ffn_a()
    nc = _L0A_CACHE["l0fa"]
    in_maps = []
    for core in range(8):
        b, par = core // 2, core % 2
        half = slice(par * 4096, (par + 1) * 4096)
        in_maps.append({
            "xlast": np.ascontiguousarray(x1[b, S - 1:S, :]),
            "in_g": inp["L0_ffn_in_gamma"].reshape(1, -1),
            "in_b": inp["L0_ffn_in_beta"].reshape(1, -1),
            "aggw": inp["L0_ffn_agg_w"],
            "aggb": inp["L0_ffn_agg_b"].reshape(1, -1),
            "png": np.ascontiguousarray(inp["L0_ffn_pn_gamma"].reshape(NC, 128).T),
            "pnb": np.ascontiguousarray(inp["L0_ffn_pn_beta"].reshape(NC, 128).T),
            "d1w": np.ascontiguousarray(inp["L0_ffn_d1_w"][:, half]),
            "d1b": np.ascontiguousarray(inp["L0_ffn_d1_b"][half].reshape(1, -1)),
            "d2w": np.ascontiguousarray(inp["L0_ffn_d2_w"][half, :]),
            "d2b": inp["L0_ffn_d2_b"].reshape(1, -1),
            "bcw": inp["L0_ffn_bc_w"],
            "bcb": inp["L0_ffn_bc_b"].reshape(1, -1),
        })
    res = run_bass_kernel_spmd(nc, in_maps, core_ids=list(range(8)))
    return np.concatenate([res.results[2 * b]["bc_out"] for b in range(B)], axis=0), res.results



def build_l0_ffn_b():
    nc = bass.Bass("TRN2", target_bir_lowering=False, debug=False)
    x_in = nc.dram_tensor("x_in", [TLOC, D], F32, kind="ExternalInput").ap()
    bc_in = nc.dram_tensor("bc_in", [1, D], F32, kind="ExternalInput").ap()
    gw = nc.dram_tensor("gw", [D, D], F32, kind="ExternalInput").ap()
    gb = nc.dram_tensor("gb", [1, D], F32, kind="ExternalInput").ap()
    outg = nc.dram_tensor("outg", [1, D], F32, kind="ExternalInput").ap()
    outb = nc.dram_tensor("outb", [1, D], F32, kind="ExternalInput").ap()
    ident_d = nc.dram_tensor("ident", [128, 128], BF16, kind="ExternalInput").ap()
    x_out = nc.dram_tensor("x_out", [TLOC, D], F32, kind="ExternalOutput").ap()

    with ExitStack() as ctx:
        E = ctx.enter_context
        P = Prog(nc, ctx)
        x_tm = E(nc.sbuf_tensor("x_tm", [128, NTT * D], F32))
        pA = E(nc.sbuf_tensor("pA", [128, NC * TLOC], BF16))
        gwb = E(nc.sbuf_tensor("gwb", [128, NC * D], BF16))
        ob = E(nc.sbuf_tensor("ob", [128, D], F32))
        gsc = E(nc.sbuf_tensor("gsc", [128, 512], F32))
        xasc = E(nc.sbuf_tensor("xasc", [128, 512], F32))
        bct = E(nc.sbuf_tensor("bct", [128, D], F32))
        ogt = E(nc.sbuf_tensor("ogt", [128, D], BF16))
        obt = E(nc.sbuf_tensor("obt", [128, D], BF16))
        gbr = E(nc.sbuf_tensor("gbr", [1, D], BF16))
        xlnt = E(nc.sbuf_tensor("xlnt", [128, D], BF16))
        stats = E(nc.sbuf_tensor("stats", [128, 256], F32))
        stats2 = E(nc.sbuf_tensor("stats2", [128, 64], F32))
        ident = E(nc.sbuf_tensor("identt", [128, 128], BF16))
        onesf = E(nc.sbuf_tensor("onesft", [1, 128], F32))
        onesbr = E(nc.sbuf_tensor("onesbr", [1, 128], BF16))
        psA = E(nc.psum_tensor("psA", [128, 512], F32))
        misc = E(nc.psum_tensor("misc", [128, 512], F32))
        trP = E(nc.psum_tensor("trP", [128, 512], BF16))
        block = E(nc.Block())

        @block.sync
        def _(sync):
            ins = sync.dma_start(out=x_tm[:, :].rearrange("p (tt d) -> p tt d", d=D),
                                 in_=x_in.rearrange("(tt p) d -> p tt d", p=128))
            P.inc(ins, "ld", 16)
            for t, d_ in ((ident, ident_d),):
                ins = sync.dma_start(out=t[:, :], in_=d_)
                P.inc(ins, "ld", 16)

        @block.gpsimd
        def _(gpsimd):
            ins = gpsimd.dma_start(out=gwb[:, :].rearrange("p (c x) -> p c x", x=D),
                                   in_=gw.rearrange("(c p) x -> p c x", p=128))
            P.inc(ins, "gld", 16)
            ins = gpsimd.dma_start(out=ogt[:, :], in_=outg.partition_broadcast(128))
            P.inc(ins, "gld", 16)
            ins = gpsimd.dma_start(out=obt[:, :], in_=outb.partition_broadcast(128))
            P.inc(ins, "gld", 16)
            ins = gpsimd.dma_start(out=bct[:, :], in_=bc_in.partition_broadcast(128))
            P.inc(ins, "gld", 16)
            ins = gpsimd.dma_start(out=gbr[:, :], in_=gb)
            P.inc(ins, "gld", 16)

        @block.vector
        def _(vector):
            P.wait(vector, "ld")
            vector.memset(onesf[:, :], 1.0)
            ins = vector.memset(onesbr[:, :], 1.0)
            P.inc(ins, "cst", 1)

        emit_ln_and_fm(P, block, x_tm, xlnt, stats, trP, ident, pA, LN_EPS,
                       tag="l0b", extra_wait="ld")

        for tt in range(NTT):
            for og in range(4):
                @block.tensor
                def _(tensor, tt=tt, og=og):
                    P.wait(tensor, "l0bfm")
                    P.wait(tensor, "gld")
                    P.wait(tensor, "cst")
                    P.wait(tensor, "gse", lag=0)
                    for c in range(NC):
                        lhsT = pA[:, c * TLOC + tt * 128:c * TLOC + (tt + 1) * 128]
                        tensor.matmul(psA[:, :], lhsT, gwb[:, c * D + og * 512:c * D + (og + 1) * 512],
                                      start=(c == 0), stop=False)
                    ins = tensor.matmul(psA[:, :], onesbr[0:1, :], gbr[0:1, og * 512:(og + 1) * 512],
                                        start=False, stop=True)
                    P.inc(ins, "gmm", 1)

                @block.scalar
                def _(scalar):
                    P.wait(scalar, "gmm")
                    P.wait(scalar, "cmb", lag=0)
                    ins = scalar.activation(out=gsc[:, :], in_=psA[:, :], func=AF.Sigmoid)
                    P.multi_inc(scalar, ins, [("sig", 1), ("gse", 1)])

                @block.vector
                def _(vector, tt=tt, og=og):
                    P.wait(vector, "sig")
                    P.wait(vector, "l0bs")
                    ag = stats[:, tt * 32 + 24:tt * 32 + 26]
                    vector.tensor_scalar(out=xasc[:, :], in0=x_tm[:, tt * D + og * 512:tt * D + (og + 1) * 512],
                                         scalar1=ag[:, 0:1], scalar2=ag[:, 1:2],
                                         op0=ALU.subtract, op1=ALU.mult).then_inc(P.sem("xa"), 1)
                    P.counts["xa"] += 1
                    vector.wait_ge(P.sem("xa"), P.counts["xa"])
                    vector.tensor_tensor(gsc[:, :], gsc[:, :], bct[:, og * 512:(og + 1) * 512],
                                         ALU.mult).then_inc(P.sem("xa"), 1)
                    P.counts["xa"] += 1
                    vector.wait_ge(P.sem("xa"), P.counts["xa"])
                    ins = vector.tensor_tensor(ob[:, og * 512:(og + 1) * 512], xasc[:, :], gsc[:, :], ALU.add)
                    P.inc(ins, "cmb", 1)

            st2 = stats2[:, 0:24]
            ag2 = stats2[:, 24:26]
            emit_ln_stats(P, block, ob[:, :], st2, ag2, FFN_EPS, "e1", pre_wait=["cmb"])

            @block.vector
            def _(vector, tt=tt):
                P.wait(vector, "e1s")
                vector.tensor_scalar(out=ob[:, :], in0=ob[:, :], scalar1=ag2[:, 0:1],
                                     scalar2=ag2[:, 1:2], op0=ALU.subtract, op1=ALU.mult).then_inc(P.sem("e1n"), 1)
                P.counts["e1n"] += 1
                vector.wait_ge(P.sem("e1n"), P.counts["e1n"])
                vector.tensor_tensor(ob[:, :], ob[:, :], ogt[:, :], ALU.mult).then_inc(P.sem("e1n"), 1)
                P.counts["e1n"] += 1
                vector.wait_ge(P.sem("e1n"), P.counts["e1n"])
                ins = vector.tensor_tensor(ob[:, :], ob[:, :], obt[:, :], ALU.add)
                P.inc(ins, "e1g", 1)

            st3 = stats2[:, 32:56]
            ag3 = stats2[:, 56:58]
            emit_ln_stats(P, block, ob[:, :], st3, ag3, LN_EPS, "e2", pre_wait=["e1g"])

            @block.vector
            def _(vector, tt=tt):
                P.wait(vector, "e2s")
                vector.tensor_scalar(out=ob[:, :], in0=ob[:, :], scalar1=ag3[:, 0:1],
                                     scalar2=ag3[:, 1:2], op0=ALU.subtract, op1=ALU.mult).then_inc(P.sem("e2n"), 1)
                P.counts["e2n"] += 1
                vector.wait_ge(P.sem("e2n"), P.counts["e2n"])
                ins = vector.tensor_tensor(x_tm[:, tt * D:(tt + 1) * D],
                                           x_tm[:, tt * D:(tt + 1) * D], ob[:, :], ALU.add)
                P.inc(ins, "e2d", 1)

            @block.sync
            def _(sync, tt=tt):
                P.wait(sync, "e2d")
                ins = sync.dma_start(out=x_out.rearrange("(tt p) d -> p tt d", p=128)[:, tt, :],
                                     in_=x_tm[:, tt * D:(tt + 1) * D])
                P.inc(ins, "xout", 16)

        @block.sync
        def _(sync):
            P.wait(sync, "xout")
            P.bump(sync, "done", 1)
    return nc


def run_l0_ffn_b(x1, bc, inp):
    if "l0fb" not in _L0A_CACHE:
        _L0A_CACHE["l0fb"] = build_l0_ffn_b()
    nc = _L0A_CACHE["l0fb"]
    in_maps = []
    for core in range(8):
        b, par = core // 2, core % 2
        cst = host_constants(par)
        in_maps.append({
            "x_in": np.ascontiguousarray(shard_rows(x1[b], par)),
            "bc_in": np.ascontiguousarray(bc[b:b + 1, :]),
            "gw": inp["L0_ffn_gate_w"],
            "gb": inp["L0_ffn_gate_b"].reshape(1, -1),
            "outg": inp["L0_ffn_out_gamma"].reshape(1, -1),
            "outb": inp["L0_ffn_out_beta"].reshape(1, -1),
            "ident": cst["ident"],
        })
    res = run_bass_kernel_spmd(nc, in_maps, core_ids=list(range(8)))
    return unshard_rows([res.results[c]["x_out"] for c in range(8)]), res.results



UPTO = 99
ATTN_S = 2
ATTN_STAGE = 9
ATTN_HG = 99


def build_l1_attn(debug=()):
    nc = bass.Bass("TRN2", target_bir_lowering=False, debug=False)
    x_in = nc.dram_tensor("x_in", [TLOC, D], F32, kind="ExternalInput").ap()
    wq = nc.dram_tensor("wq", [D, H * QK_DIM], F32, kind="ExternalInput").ap()
    wkva = nc.dram_tensor("wkva", [D, KV_LORA + QK_ROPE], F32, kind="ExternalInput").ap()
    kvnw = nc.dram_tensor("kvnw", [1, KV_LORA], F32, kind="ExternalInput").ap()
    wkvb = nc.dram_tensor("wkvb", [KV_LORA, H * (QK_NOPE + V_DIM)], F32, kind="ExternalInput").ap()
    wo = nc.dram_tensor("wo", [H * V_DIM, D], F32, kind="ExternalInput").ap()
    t1_s0 = nc.dram_tensor("t1_s0", [NJ[0], 128, 512], BF16, kind="ExternalInput").ap()
    t1_s1 = nc.dram_tensor("t1_s1", [NJ[1], 128, 512], BF16, kind="ExternalInput").ap()
    qcos_d = nc.dram_tensor("qcos", [32, TLOC], BF16, kind="ExternalInput").ap()
    qsin_d = nc.dram_tensor("qsin", [32, TLOC], BF16, kind="ExternalInput").ap()
    kcos_d = nc.dram_tensor("kcos", [128, NTT * 32], BF16, kind="ExternalInput").ap()
    ksin_d = nc.dram_tensor("ksin", [128, NTT * 32], BF16, kind="ExternalInput").ap()
    ident_d = nc.dram_tensor("ident", [128, 128], BF16, kind="ExternalInput").ap()
    x_out = nc.dram_tensor("x_out", [TLOC, D], F32, kind="ExternalOutput").ap()

    qn_loc = nc.dram_tensor("qn_loc", [H, 128, TLOC], BF16, kind="Internal").ap()
    qr_loc = nc.dram_tensor("qr_loc", [H, 64, TLOC], BF16, kind="Internal").ap()
    k_loc = nc.dram_tensor("k_loc", [8, 128, H * 128], BF16, kind="Internal").ap()
    kr_loc = nc.dram_tensor("kr_loc", [8, 64, 128], BF16, kind="Internal").ap()
    v_loc = nc.dram_tensor("v_loc", [8, 128, D], BF16, kind="Internal").ap()
    k_sh = nc.dram_tensor("k_sh", [16, 128, H * 128], BF16, kind="Internal", addr_space="Shared").ap()
    kr_sh = nc.dram_tensor("kr_sh", [16, 64, 128], BF16, kind="Internal", addr_space="Shared").ap()
    v_sh = nc.dram_tensor("v_sh", [16, 128, D], BF16, kind="Internal", addr_space="Shared").ap()
    norm_sh = nc.dram_tensor("norm_sh", [2, 1, 4], F32, kind="Internal", addr_space="Shared").ap()
    flag_sh = nc.dram_tensor("flag_sh", [2, 1], I32, kind="Internal", addr_space="Shared").ap()
    ncsc = nc.dram_tensor("ncsc", [1, 1], F32, kind="Internal").ap()

    dbg = {}
    for name, shp, dt_ in debug:
        dbg[name] = nc.dram_tensor("dbg_" + name, shp, dt_, kind="ExternalOutput").ap()

    with ExitStack() as ctx:
        E = ctx.enter_context
        P = Prog(nc, ctx)
        x_tm = E(nc.sbuf_tensor("x_tm", [128, NTT * D], F32))
        pA = E(nc.sbuf_tensor("pA", [128, NC * TLOC], BF16))
        wbig = E(nc.sbuf_tensor("wbig", [128, 16384], BF16))
        pC = E(nc.sbuf_tensor("pC", [128, H * TLOC], BF16))
        ckv = E(nc.sbuf_tensor("ckv", [128, NTT * 512], BF16))
        ckvn_fm = E(nc.sbuf_tensor("ckvn_fm", [128, 4 * TLOC], BF16))
        krtm = E(nc.sbuf_tensor("krtm", [128, NTT * 64], F32))
        krot_tm = E(nc.sbuf_tensor("krot_tm", [128, NTT * 64], BF16))
        kcs = E(nc.sbuf_tensor("kcs", [128, NTT * 32], BF16))
        ksn = E(nc.sbuf_tensor("ksn", [128, NTT * 32], BF16))
        vstg = E(nc.sbuf_tensor("vstg", [128, 2048], BF16))
        ob = E(nc.sbuf_tensor("ob", [128, D], F32))
        sq512 = E(nc.sbuf_tensor("sq512", [128, 512], BF16))
        wnt = E(nc.sbuf_tensor("wnt", [128, 512], F32))
        stats = E(nc.sbuf_tensor("stats", [128, 256], F32))
        stats2 = E(nc.sbuf_tensor("stats2", [128, 64], F32))
        ident = E(nc.sbuf_tensor("identt", [128, 128], BF16))
        onescol = E(nc.sbuf_tensor("onescol", [128, 1], BF16))
        onesf = E(nc.sbuf_tensor("onesft", [1, 128], F32))
        cvals = E(nc.sbuf_tensor("cvals", [1, 16], F32))
        negC = E(nc.sbuf_tensor("negC", [128, 1], F32))
        sums = E(nc.sbuf_tensor("sums", [1, 1024], F32))
        nrow = E(nc.sbuf_tensor("nrow", [1, 512], F32))
        rsum = E(nc.sbuf_tensor("rsum", [1, 512], F32))
        rstd_c = E(nc.sbuf_tensor("rstd_c", [128, NTT], F32))
        flag = E(nc.sbuf_tensor("flag", [1, 1], I32))
        pflag = E(nc.sbuf_tensor("pflag", [1, 1], I32))
        pnorm = E(nc.sbuf_tensor("pnorm", [1, 4], F32))

        psA = E(nc.psum_tensor("psA", [128, 1024], F32))
        psB = E(nc.psum_tensor("psB", [128, 1024], F32))
        psV = E(nc.psum_tensor("psV", [128, 1024], F32))
        misc = E(nc.psum_tensor("misc", [128, 512], F32))
        trP = E(nc.psum_tensor("trP", [128, 512], BF16))

        xlnt = wbig[:, 8192:10240]
        qcos = wbig[0:32, 0:1024]
        qsin = wbig[0:32, 1024:2048]
        qrawE = wbig[0:32, 2048:4096].bitcast(F32)
        qrawO = wbig[0:32, 4096:6144].bitcast(F32)
        wqsl = wbig[:, 13312:16384]
        qnst = wbig[:, 10240:11264]
        qrotE = wbig[0:32, 11264:12288]
        qrotO = wbig[0:32, 12288:13312]
        pbuf = pA[:, 0:2048]
        Tst = pA[:, 2048:4096]
        qstn = pA[:, 4096:5120]
        qstr = pA[0:64, 5120:6144]
        kstn = pA[:, 6144:6656]
        kstr = pA[0:64, 6656:6912]
        vstt = pA[:, 6912:7424]

        block = E(nc.Block())

        @block.sync
        def _(sync):
            ins = sync.dma_start(out=x_tm[:, :].rearrange("p (tt d) -> p tt d", d=D),
                                 in_=x_in.rearrange("(tt p) d -> p tt d", p=128))
            P.inc(ins, "ld", 16)
            ins = sync.dma_start(out=ident[:, :], in_=ident_d)
            P.inc(ins, "ld", 16)
            for t, d_ in ((kcs, kcos_d), (ksn, ksin_d)):
                ins = sync.dma_start(out=t[:, :], in_=d_)
                P.inc(ins, "ld", 16)

        @block.gpsimd
        def _(gpsimd):
            ins = gpsimd.dma_start(out=wnt[:, :], in_=kvnw.partition_broadcast(128))
            P.inc(ins, "ld", 16)
            ins = gpsimd.dma_start(out=qcos, in_=qcos_d)
            P.inc(ins, "qcs", 16)
            ins = gpsimd.dma_start(out=qsin, in_=qsin_d)
            P.inc(ins, "qcs", 16)

        @block.vector
        def _(vector):
            P.wait(vector, "ld")
            vector.memset(onesf[:, :], 1.0)
            vector.memset(onescol[:, :], 1.0)
            vector.memset(sums[:, :], 0.0)
            ins = vector.memset(cvals[:, :], 0.0)
            P.inc(ins, "cst", 1)

        emit_ln_and_fm(P, block, x_tm, xlnt, stats, trP, ident, pA, LN_EPS,
                       tag="l1a", extra_wait="ld")

        for h in range(H):
            @block.gpsimd
            def _(gpsimd, h=h):
                P.wait(gpsimd, "qmm", lag=0)
                if h == 0:
                    P.wait(gpsimd, "qcs")
                ins = gpsimd.dma_start(
                    out=wqsl[:, :].rearrange("p (c x) -> p c x", x=QK_DIM),
                    in_=wq[:, h * QK_DIM:(h + 1) * QK_DIM].rearrange("(c p) x -> p c x", p=128))
                P.inc(ins, "qwl", 16)

            @block.tensor
            def _(tensor, h=h):
                P.wait(tensor, "qwl")
                if h == 0:
                    P.wait(tensor, "l1afm")
                P.wait(tensor, "qev", lag=0)
                wv = wqsl.rearrange("p (c x) -> p c x", x=QK_DIM)
                for c in range(NC):
                    for half in range(2):
                        tensor.matmul(psA[:, half * 512:(half + 1) * 512],
                                      wv[:, c, 0:128],
                                      pA[:, c * TLOC + half * 512:c * TLOC + (half + 1) * 512],
                                      start=(c == 0), stop=(c == NC - 1))
                last = None
                for c in range(NC):
                    for half in range(2):
                        rhs = pA[:, c * TLOC + half * 512:c * TLOC + (half + 1) * 512]
                        last = tensor.matmul(psB[0:32, half * 512:(half + 1) * 512],
                                             wv[:, c, 128:192:2], rhs,
                                             start=(c == 0), stop=(c == NC - 1),
                                             skip_group_check=True)
                        last = tensor.matmul(psV[0:32, half * 512:(half + 1) * 512],
                                             wv[:, c, 129:192:2], rhs,
                                             start=(c == 0), stop=(c == NC - 1),
                                             skip_group_check=True)
                P.inc(last, "qmm", 1)

            @block.scalar
            def _(scalar, h=h):
                P.wait(scalar, "qmm")
                P.wait(scalar, "qnw", lag=0)
                scalar.copy(qnst, psA[:, :])
                scalar.copy(qrawE, psB[0:32, :])
                ins = scalar.copy(qrawO, psV[0:32, :])
                P.inc(ins, "qev", 1)

            @block.vector
            def _(vector, h=h):
                P.wait(vector, "qev")
                sE = ob[:, 0:512].bitcast(BF16)[0:32, 0:1024]
                sO = ob[:, 512:1024].bitcast(BF16)[0:32, 0:1024]
                vector.tensor_tensor(sE, qrawE, qcos, ALU.mult).then_inc(P.sem("qr1"), 1)
                P.counts["qr1"] += 1
                vector.tensor_tensor(sO, qrawO, qsin, ALU.mult).then_inc(P.sem("qr1"), 1)
                P.counts["qr1"] += 1
                vector.wait_ge(P.sem("qr1"), P.counts["qr1"])
                vector.tensor_sub(qrotE, sE, sO).then_inc(P.sem("qr2"), 1)
                P.counts["qr2"] += 1
                vector.tensor_tensor(sE, qrawO, qcos, ALU.mult).then_inc(P.sem("qr2"), 1)
                P.counts["qr2"] += 1
                vector.tensor_tensor(sO, qrawE, qsin, ALU.mult).then_inc(P.sem("qr2"), 1)
                P.counts["qr2"] += 1
                vector.wait_ge(P.sem("qr2"), P.counts["qr2"])
                ins = vector.tensor_add(qrotO, sE, sO)
                P.inc(ins, "qrot", 1)

            @block.scalar
            def _(scalar, h=h):
                P.wait(scalar, "qrot")
                P.wait(scalar, "nsq", lag=0)
                scalar.activation(out=sq512[:, :], in_=qnst[:, 0:512], func=AF.Square).then_inc(P.sem("sqr"), 1)
                P.counts["sqr"] += 1
                scalar.activation(out=vstg[:, 0:512], in_=qnst[:, 512:1024], func=AF.Square).then_inc(P.sem("sqr"), 1)
                P.counts["sqr"] += 1
                scalar.activation(out=vstg[0:32, 512:1024], in_=qrotE[0:32, 0:512], func=AF.Square).then_inc(P.sem("sqr"), 1)
                P.counts["sqr"] += 1
                scalar.activation(out=vstg[0:32, 1024:1536], in_=qrotE[0:32, 512:1024], func=AF.Square).then_inc(P.sem("sqr"), 1)
                P.counts["sqr"] += 1
                scalar.activation(out=vstg[0:32, 1536:2048], in_=qrotO[0:32, 0:512], func=AF.Square).then_inc(P.sem("sqr"), 1)
                P.counts["sqr"] += 1
                scalar.activation(out=pC[0:32, 0:512], in_=qrotO[0:32, 512:1024], func=AF.Square).then_inc(P.sem("sqr"), 1)
                P.counts["sqr"] += 1

            for half in range(2):
                @block.tensor
                def _(tensor, half=half):
                    P.wait(tensor, "sqr")
                    P.wait(tensor, "nrd", lag=0)
                    if half == 0:
                        sqn, sqe, sqo = sq512[:, :], vstg[0:32, 512:1024], vstg[0:32, 1536:2048]
                    else:
                        sqn, sqe, sqo = vstg[:, 0:512], vstg[0:32, 1024:1536], pC[0:32, 0:512]
                    tensor.matmul(misc[0:1, 0:512], onescol[:, 0:1], sqn, start=True, stop=False)
                    tensor.matmul(misc[0:1, 0:512], onescol[0:32, 0:1], sqe, start=False, stop=False)
                    ins = tensor.matmul(misc[0:1, 0:512], onescol[0:32, 0:1], sqo, start=False, stop=True)
                    P.inc(ins, "nmm", 1)

                @block.vector
                def _(vector, half=half):
                    P.wait(vector, "nmm")
                    vector.tensor_reduce(nrow[0:1, 0:1], misc[0:1, 0:512], AX.X, ALU.max).then_inc(P.sem("nrd"), 1)
                    P.counts["nrd"] += 1
                    vector.wait_ge(P.sem("nrd"), P.counts["nrd"])
                    ins = vector.tensor_max(cvals[:, 0:1], cvals[:, 0:1], nrow[0:1, 0:1])
                    nm = "nmx" if half == 1 else "nmxh"
                    P.multi_inc(vector, ins, [(nm, 1), ("nsq" if half == 1 else "nsqh", 1)])

            @block.sync
            def _(sync, h=h):
                P.wait(sync, "nmx")
                ins = sync.dma_start(out=qn_loc[h, :, :], in_=qnst)
                P.inc(ins, "qnw", 16)
                ins = sync.dma_start(out=qr_loc[h, 0:32, :], in_=qrotE)
                P.inc(ins, "qnw", 16)
                ins = sync.dma_start(out=qr_loc[h, 32:64, :], in_=qrotO)
                P.inc(ins, "qnw", 16)

        if UPTO <= 1:
            return nc
        wkvab = wbig[:, 0:9216]

        @block.gpsimd
        def _(gpsimd):
            P.wait(gpsimd, "qnw")
            ins = gpsimd.dma_start(out=wkvab.rearrange("p (c x) -> p c x", x=576),
                                   in_=wkva.rearrange("(c p) x -> p c x", p=128))
            P.inc(ins, "kal", 16)

        for tt in range(NTT):
            @block.tensor
            def _(tensor, tt=tt):
                P.wait(tensor, "kal")
                P.wait(tensor, "kaev", lag=0)
                if tt == 0:
                    P.wait(tensor, "qev")
                    P.wait(tensor, "nmx")
                    P.wait(tensor, "qnw")
                wv = wkvab.rearrange("p (c x) -> p c x", x=576)
                for c in range(NC):
                    lhsT = pA[:, c * TLOC + tt * 128:c * TLOC + (tt + 1) * 128]
                    tensor.matmul(psA[:, 0:512], lhsT, wv[:, c, 0:512],
                                  start=(c == 0), stop=(c == NC - 1))
                    ins = tensor.matmul(misc[:, 0:64], lhsT, wv[:, c, 512:576],
                                        start=(c == 0), stop=(c == NC - 1))
                P.inc(ins, "kamm", 1)

            @block.scalar
            def _(scalar, tt=tt):
                P.wait(scalar, "kamm")
                scalar.copy(ckv[:, tt * 512:(tt + 1) * 512], psA[:, 0:512])
                ins = scalar.copy(krtm[:, tt * 64:(tt + 1) * 64], misc[:, 0:64])
                P.inc(ins, "kaev", 1)

            @block.vector
            def _(vector, tt=tt):
                P.wait(vector, "kaev")
                ct = ckv[:, tt * 512:(tt + 1) * 512]
                scr = ob[:, 0:512]
                vector.tensor_tensor(scr, ct, ct, ALU.mult).then_inc(P.sem("rm1"), 1)
                P.counts["rm1"] += 1
                vector.wait_ge(P.sem("rm1"), P.counts["rm1"])
                vector.tensor_reduce(ob[:, 512:513], scr, AX.X, ALU.add).then_inc(P.sem("rm1"), 1)
                P.counts["rm1"] += 1
                vector.wait_ge(P.sem("rm1"), P.counts["rm1"])
                vector.tensor_scalar(out=ob[:, 513:514], in0=ob[:, 512:513],
                                     scalar1=1.0 / KV_LORA, scalar2=float(RMS_EPS),
                                     op0=ALU.mult, op1=ALU.add).then_inc(P.sem("rm2"), 1)
                P.counts["rm2"] += 1

            @block.scalar
            def _(scalar, tt=tt):
                P.wait(scalar, "rm2")
                scalar.activation(out=ob[:, 514:515], in_=ob[:, 513:514], func=AF.Ln).then_inc(P.sem("rm3"), 1)
                P.counts["rm3"] += 1
                scalar.wait_ge(P.sem("rm3"), P.counts["rm3"])
                ins = scalar.activation(out=rstd_c[:, tt:tt + 1], in_=ob[:, 514:515], func=AF.Exp, scale=-0.5)
                P.inc(ins, "rm4", 1)

            @block.vector
            def _(vector, tt=tt):
                P.wait(vector, "rm4")
                ct = ckv[:, tt * 512:(tt + 1) * 512]
                ins = vector.scalar_tensor_tensor(ct, ct, rstd_c[:, tt:tt + 1], wnt[:, :],
                                                  op0=ALU.mult, op1=ALU.mult)
                P.inc(ins, "rmsd", 1)

            @block.vector
            def _(vector, tt=tt):
                P.wait(vector, "rmsd")
                kt = krtm[:, tt * 64:(tt + 1) * 64]
                x1 = kt[:, 0:64:2]
                x2 = kt[:, 1:64:2]
                cosv = kcs[:, tt * 32:(tt + 1) * 32]
                sinv = ksn[:, tt * 32:(tt + 1) * 32]
                s1 = ob[:, 0:32]
                s2 = ob[:, 32:64]
                ko = krot_tm[:, tt * 64:(tt + 1) * 64]
                vector.tensor_tensor(s1, x1, cosv, ALU.mult).then_inc(P.sem("kr1"), 1)
                P.counts["kr1"] += 1
                vector.tensor_tensor(s2, x2, sinv, ALU.mult).then_inc(P.sem("kr1"), 1)
                P.counts["kr1"] += 1
                vector.wait_ge(P.sem("kr1"), P.counts["kr1"])
                vector.tensor_sub(ko[:, 0:32], s1, s2).then_inc(P.sem("kr1"), 1)
                P.counts["kr1"] += 1
                vector.tensor_tensor(s1, x2, cosv, ALU.mult).then_inc(P.sem("kr1"), 1)
                P.counts["kr1"] += 1
                vector.tensor_tensor(s2, x1, sinv, ALU.mult).then_inc(P.sem("kr1"), 1)
                P.counts["kr1"] += 1
                vector.wait_ge(P.sem("kr1"), P.counts["kr1"])
                ins = vector.tensor_add(ko[:, 32:64], s1, s2)
                P.inc(ins, "krot", 1)

        for tt in range(NTT):
            @block.tensor
            def _(tensor, tt=tt):
                P.wait(tensor, "rmsd")
                P.wait(tensor, "ctfe", lag=0)
                P.wait(tensor, "krfe", lag=0)
                for c4 in range(4):
                    ins = tensor.transpose(trP[:, c4 * 128:(c4 + 1) * 128],
                                           ckv[:, tt * 512 + c4 * 128:tt * 512 + (c4 + 1) * 128],
                                           ident[:, :])
                P.inc(ins, "ctf", 1)

            @block.scalar
            def _(scalar, tt=tt):
                P.wait(scalar, "ctf")
                for c4 in range(4):
                    ins = scalar.copy(ckvn_fm[:, c4 * TLOC + tt * 128:c4 * TLOC + (tt + 1) * 128],
                                      trP[:, c4 * 128:(c4 + 1) * 128])
                P.inc(ins, "ctfe", 1)

            @block.tensor
            def _(tensor, tt=tt):
                P.wait(tensor, "krot")
                P.wait(tensor, "krfe", lag=0)
                P.wait(tensor, "ctfe")
                ins = tensor.transpose(trP[0:64, 0:128],
                                       krot_tm[:, tt * 64:(tt + 1) * 64], ident[:, :])
                P.inc(ins, "krf", 1)

            @block.vector
            def _(vector, tt=tt):
                P.wait(vector, "krf")
                P.wait(vector, "krw", lag=0)
                ins = vector.tensor_copy(vstg[0:64, 0:128], trP[0:64, 0:128])
                P.inc(ins, "krfe", 1)

            @block.sync
            def _(sync, tt=tt):
                P.wait(sync, "krfe")
                ins = sync.dma_start(out=kr_loc[tt, :, :], in_=vstg[0:64, 0:128])
                P.inc(ins, "krw", 16)

        if UPTO <= 2:
            return nc
        wkvbb = wbig[:, 0:16384]
        pc4 = pC[:, :].rearrange("p (blk h t) -> p blk h t", blk=8, h=H)

        @block.gpsimd
        def _(gpsimd):
            P.wait(gpsimd, "kamm")
            ins = gpsimd.dma_start(out=wkvbb.rearrange("p (c x) -> p c x", x=4096),
                                   in_=wkvb.rearrange("(c p) x -> p c x", p=128))
            P.inc(ins, "kbl", 16)

        wv4 = wkvbb.rearrange("p (c x) -> p c x", x=4096)
        for h in range(H):
            @block.tensor
            def _(tensor, h=h):
                P.wait(tensor, "kbl")
                P.wait(tensor, "ctfe")
                P.wait(tensor, "knev", lag=0)
                P.wait(tensor, "knsr", lag=0)
                if h == 0:
                    P.wait(tensor, "kaev")
                for c in range(4):
                    lhsT = wv4[:, c, h * 256:h * 256 + 128]
                    tensor.matmul(psA[:, 0:512], lhsT, ckvn_fm[:, c * TLOC:c * TLOC + 512],
                                  start=(c == 0), stop=(c == 3))
                    ins = tensor.matmul(psA[:, 512:1024], lhsT, ckvn_fm[:, c * TLOC + 512:(c + 1) * TLOC],
                                        start=(c == 0), stop=(c == 3))
                P.inc(ins, "knmm", 1)

            @block.scalar
            def _(scalar, h=h):
                P.wait(scalar, "knmm")
                for s_ in range(2):
                    src = psA[:, s_ * 512:(s_ + 1) * 512].rearrange("p (b t) -> p b t", b=4)
                    ins = scalar.activation(out=pc4[:, s_ * 4:(s_ + 1) * 4, h, :], in_=src, func=AF.Copy)
                P.inc(ins, "knev", 1)

            @block.scalar
            def _(scalar, h=h):
                P.wait(scalar, "knev")
                P.wait(scalar, "knsq", lag=0)
                if h == 0:
                    P.wait(scalar, "krw")
                scalar.activation(out=sq512[:, :], in_=psA[:, 0:512], func=AF.Square).then_inc(P.sem("knsr"), 1)
                P.counts["knsr"] += 1
                scalar.wait_ge(P.sem("knsr"), P.counts["knsr"])
                scalar.activation(out=vstg[:, 0:512], in_=psA[:, 512:1024], func=AF.Square).then_inc(P.sem("knsr"), 1)
                P.counts["knsr"] += 1

            for half in range(2):
                @block.tensor
                def _(tensor, half=half):
                    P.wait(tensor, "knsr")
                    P.wait(tensor, "knrd", lag=0)
                    sqn = sq512[:, :] if half == 0 else vstg[:, 0:512]
                    ins = tensor.matmul(misc[0:1, 0:512], onescol[:, 0:1], sqn, start=True, stop=True)
                    P.inc(ins, "knm2", 1)

                @block.vector
                def _(vector, half=half):
                    P.wait(vector, "knm2")
                    vector.tensor_reduce(nrow[0:1, 0:1], misc[0:1, 0:512], AX.X, ALU.max).then_inc(P.sem("knrd"), 1)
                    P.counts["knrd"] += 1
                    vector.wait_ge(P.sem("knrd"), P.counts["knrd"])
                    ins = vector.tensor_max(cvals[:, 1:2], cvals[:, 1:2], nrow[0:1, 0:1])
                    nm = "knmx" if half == 1 else "knmxh"
                    P.multi_inc(vector, ins, [(nm, 1), ("knsq" if half == 1 else "knsqh", 1)])

        for tt in range(NTT):
            for hh in range(2):
                @block.tensor
                def _(tensor, tt=tt, hh=hh):
                    P.wait(tensor, "kbl")
                    P.wait(tensor, "ctfe")
                    P.wait(tensor, "knmx")
                    P.wait(tensor, "vev", lag=0)
                    for hi in range(8):
                        h = hh * 8 + hi
                        for c in range(4):
                            lhsT = ckvn_fm[:, c * TLOC + tt * 128:c * TLOC + (tt + 1) * 128]
                            ins = tensor.matmul(psV[:, hi * 128:(hi + 1) * 128], lhsT,
                                                wv4[:, c, h * 256 + 128:(h + 1) * 256],
                                                start=(c == 0), stop=(c == 3))
                    P.inc(ins, "vmm", 1)

                @block.vector
                def _(vector, tt=tt, hh=hh):
                    P.wait(vector, "vmm")
                    P.wait(vector, "krw")
                    if hh == 0:
                        P.wait(vector, "vwr", lag=0)
                    ins = vector.tensor_copy(vstg[:, hh * 1024:(hh + 1) * 1024], psV[:, :])
                    P.inc(ins, "vev", 1)

            @block.sync
            def _(sync, tt=tt):
                P.wait(sync, "vev")
                ins = sync.dma_start(out=v_loc[tt, :, :], in_=vstg[:, :])
                P.inc(ins, "vwr", 16)

        @block.sync
        def _(sync):
            P.wait(sync, "knev")
            ins = sync.dma_start(out=k_loc[:, :, :].rearrange("blk p x -> p blk x"),
                                 in_=pC[:, :].rearrange("p (blk x) -> p blk x", blk=8))
            P.inc(ins, "kwr", 16)

        if UPTO <= 3:
            return nc
        @block.gpsimd
        def _(gpsimd):
            P.wait(gpsimd, "knmx")
            P.wait(gpsimd, "nmx")
            ins = gpsimd.memset(flag[:, :], MAGIC + 10)
            P.inc(ins, "flg", 1)

        @block.sync
        def _(sync):
            P.wait(sync, "flg")
            P.wait(sync, "kwr")
            P.wait(sync, "vwr")
            P.wait(sync, "krw")
            b = sync.bass
            engs = [sync.engine]
            regs = b.alloc_registers("parsw1", engines=engs)
            pid = sync.partition_id()
            b.regs_mov(regs, pid % 2)
            parsv = b.snap(regs, engines=engs, min_val=0, max_val=1)
            pre = dict(P.counts)
            deltas = []
            for arm in sync.Switch(parsv, 2):
                ch = CHUNKS[arm]
                for sl in range(2):
                    c = ch[sl]
                    ins = sync.dma_start(out=k_sh[c * 4:(c + 1) * 4, :, :], in_=k_loc[sl * 4:(sl + 1) * 4, :, :])
                    P.inc(ins, "xch", 16)
                    ins = sync.dma_start(out=kr_sh[c * 4:(c + 1) * 4, :, :], in_=kr_loc[sl * 4:(sl + 1) * 4, :, :])
                    P.inc(ins, "xch", 16)
                    ins = sync.dma_start(out=v_sh[c * 4:(c + 1) * 4, :, :], in_=v_loc[sl * 4:(sl + 1) * 4, :, :])
                    P.inc(ins, "xch", 16)
                ins = sync.dma_start(out=norm_sh[arm, :, :], in_=cvals[:, 0:4])
                P.inc(ins, "xch", 16)
                P.wait(sync, "xch")
                ins = sync.dma_start(out=flag_sh[arm:arm + 1, :].squeeze(0), in_=flag[0, :])
                P.inc(ins, "xch", 16)
                P.wait(sync, "xch")
                deltas.append({k: P.counts[k] - pre.get(k, 0) for k in P.counts})
                if arm == 0:
                    P.counts = dict(pre)
            assert deltas[0] == deltas[1]
            P.counts = {k: pre.get(k, 0) + deltas[0].get(k, 0) for k in set(pre) | set(deltas[0])}

        @block.scalar
        def _(scalar):
            pid = scalar.partition_id()
            other = scalar.compute_val(1 - pid % 2)
            spin_poll(P, scalar, pflag, flag_sh[bass.ds(other, 1), :].squeeze(0), MAGIC + 10, "l1kv")
            ins = scalar.dma_start(out=pnorm[:, :], in_=norm_sh[bass.ds(other, 1), :, :].squeeze(0))
            P.inc(ins, "pnr", 16)
            P.wait(scalar, "pnr")
            P.bump(scalar, "pnd", 1)

        @block.vector
        def _(vector):
            P.wait(vector, "pnd")
            ins = vector.tensor_max(cvals[:, 1:2], cvals[:, 1:2], pnorm[:, 1:2])
            P.inc(ins, "cv1", 1)

        @block.scalar
        def _(scalar):
            P.wait(scalar, "cv1")
            scalar.activation(out=cvals[:, 2:3], in_=cvals[:, 0:1], func=AF.Ln).then_inc(P.sem("cv2"), 1)
            P.counts["cv2"] += 1
            scalar.wait_ge(P.sem("cv2"), P.counts["cv2"])
            scalar.activation(out=cvals[:, 3:4], in_=cvals[:, 1:2], func=AF.Ln).then_inc(P.sem("cv2"), 1)
            P.counts["cv2"] += 1

        @block.vector
        def _(vector):
            P.wait(vector, "cv2")
            ins = vector.tensor_add(cvals[:, 4:5], cvals[:, 2:3], cvals[:, 3:4])
            P.inc(ins, "cv3", 1)

        @block.scalar
        def _(scalar):
            P.wait(scalar, "cv3")
            ins = scalar.activation(out=cvals[:, 5:6], in_=cvals[:, 4:5], func=AF.Exp, scale=0.5)
            P.inc(ins, "cv4", 1)

        @block.vector
        def _(vector):
            P.wait(vector, "cv4")
            vector.tensor_scalar(out=cvals[:, 6:7], in0=cvals[:, 5:6], scalar1=SCALE1,
                                 scalar2=43.0, op0=ALU.mult, op1=ALU.min).then_inc(P.sem("cv5"), 1)
            P.counts["cv5"] += 1
            vector.wait_ge(P.sem("cv5"), P.counts["cv5"])
            ins = vector.tensor_scalar_mul(cvals[:, 7:8], cvals[:, 6:7], -1.0)
            P.inc(ins, "cv6", 1)

        @block.sync
        def _(sync):
            P.wait(sync, "cv6")
            ins = sync.dma_start(out=ncsc, in_=cvals[0:1, 7:8])
            P.inc(ins, "ncb0", 16)
            P.wait(sync, "ncb0")
            ins = sync.dma_start(out=negC[:, :], in_=ncsc.partition_broadcast(128))
            P.inc(ins, "ncb", 16)

        if UPTO <= 4:
            return nc
        t1_d = (t1_s0, t1_s1)
        for s in range(2):
            if s >= ATTN_S:
                break
            for hg in range(H // 2):
                if hg >= ATTN_HG:
                    break
                h0, h1 = 2 * hg, 2 * hg + 1
                njs = NJ[s]

                @block.sync
                def _(sync, s=s, h0=h0, hg=hg):
                    if s == 0 and hg == 0:
                        P.wait(sync, "ncb")
                    P.wait(sync, "qfree", lag=0)
                    for hi in range(2):
                        ins = sync.dma_start(out=qstn[:, hi * 512:(hi + 1) * 512],
                                             in_=qn_loc[h0 + hi, :, s * 512:(s + 1) * 512])
                        P.inc(ins, "qld", 16)
                        ins = sync.dma_start(out=qstr[0:64, hi * 512:(hi + 1) * 512],
                                             in_=qr_loc[h0 + hi, :, s * 512:(s + 1) * 512])
                        P.inc(ins, "qld", 16)

                for jg in range(njs // 2):
                    jj = (2 * jg, 2 * jg + 1)

                    @block.sync
                    def _(sync, s=s, jj=jj, h0=h0):
                        P.wait(sync, "kvfree", lag=0)
                        for ji, j in enumerate(jj):
                            ins = sync.dma_start(out=kstn[:, ji * 256:(ji + 1) * 256],
                                                 in_=k_sh[j, :, h0 * 128:(h0 + 2) * 128])
                            P.inc(ins, "kld", 16)
                            ins = sync.dma_start(out=kstr[0:64, ji * 128:(ji + 1) * 128],
                                                 in_=kr_sh[j, :, :])
                            P.inc(ins, "kld", 16)
                            ins = sync.dma_start(out=vstt[:, ji * 256:(ji + 1) * 256],
                                                 in_=v_sh[j, :, h0 * 128:(h0 + 2) * 128])
                            P.inc(ins, "kld", 16)

                    @block.gpsimd
                    def _(gpsimd, s=s, jj=jj):
                        P.wait(gpsimd, "Tfree", lag=0)
                        for ji, j in enumerate(jj):
                            if (s, j) in MASKED:
                                ins = gpsimd.dma_start(out=Tst[:, ji * 512:(ji + 1) * 512],
                                                       in_=t1_d[s][j, :, :])
                                P.inc(ins, "Tld", 16)

                    if ATTN_STAGE < 2:
                        continue

                    @block.tensor
                    def _(tensor, s=s, jj=jj, h0=h0, h1=h1, jg=jg):
                        P.wait(tensor, "kld")
                        P.wait(tensor, "qld")
                        P.wait(tensor, "sfree", lag=0)
                        for hi, ps in ((0, psA), (1, psB)):
                            for ji, j in enumerate(jj):
                                tensor.matmul(ps[:, ji * 512:(ji + 1) * 512],
                                              kstn[:, ji * 256 + hi * 128:ji * 256 + (hi + 1) * 128],
                                              qstn[:, hi * 512:(hi + 1) * 512],
                                              start=True, stop=False)
                                ins = tensor.matmul(ps[:, ji * 512:(ji + 1) * 512],
                                                    kstr[0:64, ji * 128:(ji + 1) * 128],
                                                    qstr[0:64, hi * 512:(hi + 1) * 512],
                                                    start=False, stop=True)
                        P.inc(ins, "smm", 1)

                    if ATTN_STAGE < 3:
                        continue

                    @block.scalar
                    def _(scalar):
                        P.wait(scalar, "smm")
                        P.wait(scalar, "pb2f")
                        scalar.activation(out=pbuf[:, 0:1024], in_=psA[:, :], func=AF.Exp,
                                          bias=negC[:, 0:1], scale=SCALE1)
                        ins = scalar.activation(out=pbuf[:, 1024:2048], in_=psB[:, :], func=AF.Exp,
                                                bias=negC[:, 0:1], scale=SCALE1)
                        P.multi_inc(scalar, ins, [("exp", 1), ("sfree", 1)])

                    if ATTN_STAGE < 4:
                        continue

                    @block.gpsimd
                    def _(gpsimd, s=s, jj=jj):
                        P.wait(gpsimd, "exp")
                        P.wait(gpsimd, "Tld")
                        P.wait(gpsimd, "sumz")
                        ins = None
                        for hi in range(2):
                            for ji, j in enumerate(jj):
                                if (s, j) in MASKED:
                                    src = pbuf[:, hi * 1024 + ji * 512:hi * 1024 + (ji + 1) * 512]
                                    tt_ = Tst[:, ji * 512:(ji + 1) * 512]
                                    ins = gpsimd.tensor_tensor(src, src, tt_, ALU.mult)
                        if ins is None:
                            ins = gpsimd.nop()
                        P.inc(ins, "msk", 1)
                        gpsimd.wait_ge(P.sem("msk"), P.counts["msk"])
                        gpsimd.sem_inc(P.sem("Tfree"), 1)
                        P.counts["Tfree"] = P.counts.get("Tfree", 0) + 1
                        for hi in range(2):
                            for ji, j in enumerate(jj):
                                src = pbuf[:, hi * 1024 + ji * 512:hi * 1024 + (ji + 1) * 512]
                                gpsimd.tensor_reduce(nrow[0:1, :], src, AX.C, ALU.add).then_inc(P.sem("red"), 1)
                                P.counts["red"] += 1
                                gpsimd.wait_ge(P.sem("red"), P.counts["red"])
                                dsum = sums[0:1, hi * 512:(hi + 1) * 512]
                                gpsimd.tensor_tensor(dsum, dsum, nrow[0:1, :], ALU.add).then_inc(P.sem("red"), 1)
                                P.counts["red"] += 1
                                gpsimd.wait_ge(P.sem("red"), P.counts["red"])
                        P.inc(gpsimd.nop(), "rdd", 1)

                    if ATTN_STAGE < 5:
                        continue

                    @block.tensor
                    def _(tensor, jj=jj, jg=jg, njs=njs):
                        P.wait(tensor, "rdd")
                        if jg == 0:
                            P.wait(tensor, "cev")
                        for hi in range(2):
                            for ji, j in enumerate(jj):
                                lhsT = vstt[:, ji * 256 + hi * 128:ji * 256 + (hi + 1) * 128]
                                ins = tensor.matmul(psV[:, hi * 512:(hi + 1) * 512], lhsT,
                                                    pbuf[:, hi * 1024 + ji * 512:hi * 1024 + (ji + 1) * 512],
                                                    start=(jg == 0 and ji == 0),
                                                    stop=(jg == njs // 2 - 1 and ji == 1))
                        P.multi_inc(tensor, ins, [("cmm", 1), ("pb2f", 1), ("kvfree", 1)])

                if ATTN_STAGE < 6:
                    continue
                for hi, h in enumerate((h0, h1)):
                    @block.vector
                    def _(vector, hi=hi):
                        P.wait(vector, "cmm")
                        P.wait(vector, "rte")
                        ins = vector.reciprocal(rsum[0:1, :], sums[0:1, hi * 512:(hi + 1) * 512])
                        P.inc(ins, "rcp", 1)

                    @block.tensor
                    def _(tensor):
                        P.wait(tensor, "rcp")
                        P.wait(tensor, "rtev", lag=0)
                        ins = tensor.matmul(misc[:, :], onesf[:, :], rsum[0:1, :], start=True, stop=True)
                        P.inc(ins, "rmm", 1)

                    @block.scalar
                    def _(scalar):
                        P.wait(scalar, "rmm")
                        ins = scalar.copy(ob[:, 0:512].bitcast(F32) if False else ob[:, 0:512], misc[:, :])
                        P.inc(ins, "rte", 1)

                    @block.vector
                    def _(vector, s=s, h=h, hi=hi):
                        P.wait(vector, "rte")
                        dst = pC[:, h * TLOC + s * 512:h * TLOC + (s + 1) * 512]
                        vector.scalar_tensor_tensor(dst, psV[:, hi * 512:(hi + 1) * 512], 1.0,
                                                    ob[:, 0:512], op0=ALU.mult, op1=ALU.mult).then_inc(P.sem("cev"), 1)
                        P.counts["cev"] += 1
                        vector.wait_ge(P.sem("cev"), P.counts["cev"])
                        ins = vector.memset(sums[0:1, hi * 512:(hi + 1) * 512], 0.0)
                        P.multi_inc(vector, ins, [("rtev", 1), ("sumz", 1)])

                @block.sync
                def _(sync):
                    P.wait(sync, "cev")
                    P.bump(sync, "qfree", 1)

        if "kstg" in dbg:
            @block.gpsimd
            def _(gpsimd):
                P.wait(gpsimd, "knev")
                ins = gpsimd.dma_start(out=dbg["kstg"], in_=pC[:, :])
                P.inc(ins, "dbgk2", 16)
        if "ckv" in dbg:
            @block.gpsimd
            def _(gpsimd):
                P.wait(gpsimd, "rmsd")
                ins = gpsimd.dma_start(out=dbg["ckv"], in_=ckv[:, :])
                P.inc(ins, "dbgc2", 16)
                gpsimd.wait_ge(P.sem("dbgc2"), P.counts["dbgc2"])
                P.wait(gpsimd, "kaev")
                ins = gpsimd.dma_start(out=dbg["krtm"], in_=krtm[:, :])
                P.inc(ins, "dbgc2", 16)
        if "kl0" in dbg:
            @block.gpsimd
            def _(gpsimd):
                P.wait(gpsimd, "kwr")
                ins = gpsimd.dma_start(out=dbg["kl0"], in_=k_loc[0, :, :])
                P.inc(ins, "dbgk", 16)
                gpsimd.wait_ge(P.sem("dbgk"), P.counts["dbgk"])
                if "vl0" in dbg:
                    P.wait(gpsimd, "vwr")
                    ins = gpsimd.dma_start(out=dbg["vl0"], in_=v_loc[0, :, :])
                    P.inc(ins, "dbgk", 16)
                    gpsimd.wait_ge(P.sem("dbgk"), P.counts["dbgk"])
                    P.wait(gpsimd, "krw")
                    ins = gpsimd.dma_start(out=dbg["kr0"], in_=kr_loc[0, :, :])
                    P.inc(ins, "dbgk", 16)
        if "qn0" in dbg:
            @block.gpsimd
            def _(gpsimd):
                P.wait(gpsimd, "qnw")
                ins = gpsimd.dma_start(out=dbg["qn0"], in_=qn_loc[0, :, :])
                P.inc(ins, "dbgn", 16)
                gpsimd.wait_ge(P.sem("dbgn"), P.counts["dbgn"])
                ins = gpsimd.dma_start(out=dbg["qr0"], in_=qr_loc[0, :, :])
                P.inc(ins, "dbgn", 16)
        if "cvals" in dbg:
            @block.gpsimd
            def _(gpsimd):
                P.wait(gpsimd, "ncb")
                ins = gpsimd.dma_start(out=dbg["cvals"], in_=cvals[:, :])
                P.inc(ins, "dbgv", 16)
        if "ctx" in dbg:
            @block.gpsimd
            def _(gpsimd):
                P.wait(gpsimd, "cev")
                ins = gpsimd.dma_start(out=dbg["ctx"], in_=pC[:, :])
                P.inc(ins, "dbgc", 16)

        if UPTO <= 5:
            return nc
        wob = wbig[:, 0:8192]
        for og in range(4):
            @block.gpsimd
            def _(gpsimd, og=og):
                P.wait(gpsimd, "cev")
                P.wait(gpsimd, "omm", lag=0)
                ins = gpsimd.dma_start(out=wob.rearrange("p (c x) -> p c x", x=512),
                                       in_=wo[:, og * 512:(og + 1) * 512].rearrange("(c p) x -> p c x", p=128))
                P.inc(ins, "wol", 16)

            for tt in range(NTT):
                s_, tl = tt // 4, tt % 4

                @block.tensor
                def _(tensor, og=og, s_=s_, tl=tl, tt=tt):
                    P.wait(tensor, "wol")
                    P.wait(tensor, "oevv", lag=0)
                    wv = wob.rearrange("p (c x) -> p c x", x=512)
                    for c in range(NC):
                        lhsT = pC[:, c * TLOC + s_ * 512 + tl * 128:c * TLOC + s_ * 512 + (tl + 1) * 128]
                        ins = tensor.matmul(misc[:, :], lhsT, wv[:, c, :],
                                            start=(c == 0), stop=(c == NC - 1))
                    P.inc(ins, "omm", 1)

                @block.scalar
                def _(scalar, og=og, tt=tt):
                    P.wait(scalar, "omm")
                    ins = scalar.copy(pA[:, tt * 2048 + og * 512:tt * 2048 + (og + 1) * 512], misc[:, :])
                    P.inc(ins, "oevv", 1)

        for tt in range(NTT):
            at = pA[:, tt * 2048:(tt + 1) * 2048]
            st = stats2[:, 0:24]
            ag = stats2[:, 24:26]
            emit_ln_stats(P, block, at, st, ag, LN_EPS, "ep2",
                          pre_wait=["oevv"] if tt == 0 else ["epd2"])

            @block.vector
            def _(vector, at=at, ag=ag, tt=tt):
                P.wait(vector, "ep2s")
                vector.tensor_scalar(out=ob[:, :], in0=at, scalar1=ag[:, 0:1],
                                     scalar2=ag[:, 1:2], op0=ALU.subtract, op1=ALU.mult).then_inc(P.sem("epn2"), 1)
                P.counts["epn2"] += 1
                vector.wait_ge(P.sem("epn2"), P.counts["epn2"])
                ins = vector.tensor_tensor(x_tm[:, tt * D:(tt + 1) * D],
                                           x_tm[:, tt * D:(tt + 1) * D], ob[:, :], ALU.add)
                P.inc(ins, "epd2", 1)

            @block.sync
            def _(sync, tt=tt):
                P.wait(sync, "epd2")
                ins = sync.dma_start(out=x_out.rearrange("(tt p) d -> p tt d", p=128)[:, tt, :],
                                     in_=x_tm[:, tt * D:(tt + 1) * D])
                P.inc(ins, "xout", 16)

        @block.sync
        def _(sync):
            P.wait(sync, "xout")
            P.bump(sync, "done", 1)
    return nc


def run_l1_attn(x2, inp, debug=()):
    key = ("l1a",) + tuple(d[0] for d in debug)
    if key not in _L0A_CACHE:
        _L0A_CACHE[key] = build_l1_attn(debug)
    nc = _L0A_CACHE[key]
    in_maps = []
    for core in range(8):
        b, par = core // 2, core % 2
        cst = host_constants(par)
        in_maps.append({
            "x_in": np.ascontiguousarray(shard_rows(x2[b], par)),
            "wq": inp["L1_attn_Wq"],
            "wkva": inp["L1_attn_Wkv_a"],
            "kvnw": inp["L1_attn_kv_norm_w"].reshape(1, -1),
            "wkvb": inp["L1_attn_Wkv_b"],
            "wo": inp["L1_attn_Wo"],
            "t1_s0": cst["t1_s0"], "t1_s1": cst["t1_s1"],
            "qcos": cst["q_cos"], "qsin": cst["q_sin"],
            "kcos": cst["k_cos"], "ksin": cst["k_sin"],
            "ident": cst["ident"],
        })
    res = run_bass_kernel_spmd(nc, in_maps, core_ids=list(range(8)))
    return unshard_rows([res.results[c]["x_out"] for c in range(8)]), res.results



def kernel(**inputs):
    x1, _ = run_l0_attn(inputs["L0_attn_x"], inputs["L0_attn_Wqkv"],
                        inputs["L0_attn_b_qkv"], inputs["L0_attn_Wout"],
                        inputs["L0_attn_b_out"])
    bc, _ = run_l0_ffn_a(x1, inputs)
    x2, _ = run_l0_ffn_b(x1, bc, inputs)
    x3, _ = run_l1_attn(x2, inputs)
    x4, _ = run_l1_ffn(x3, inputs["L1_ffn_factors"], inputs["L1_ffn_bias"])
    return np.ascontiguousarray(x4, dtype=np.float32)
